# version 04
# baseline (speedup 1.0000x reference)
import numpy as np
import ml_dtypes
import concourse.bass as bass
import concourse.mybir as mybir
from concourse.bass_utils import run_bass_kernel_spmd

F32 = mybir.dt.float32
BF16 = mybir.dt.bfloat16
U32 = mybir.dt.uint32
AF = mybir.ActivationFunctionType
ALU = mybir.AluOpType
AXX = mybir.AxisListType.X

B, S, D = 4, 4096, 1024
TOK = 2048
NT = TOK // 128
NB = S // 512
MAGIC = 1.5 * 2.0 ** 23
EPS = 1e-5
STAGE = 3
RACE_DETECT = False

ENGINES = ("sync", "gpsimd", "tensor", "vector", "scalar")
ESEM = {"sync": "sSN", "gpsimd": "sGN", "tensor": "sPE",
        "vector": "sDVE", "scalar": "sACT"}


class Prog:

    def __init__(self):
        self.ops = {e: [] for e in ENGINES}
        self.c = {}
        self.pending = {e: {} for e in ENGINES}
        self.last = {}

    def val(self, sem):
        return self.c.get(sem, 0)

    SERIAL = ("vector", "scalar")

    def add(self, eng, fn, waits=(), inc=None, by=1, selfinc=False):
        w = dict(self.pending[eng])
        self.pending[eng] = {}
        for s, v in waits:
            w[s] = max(w.get(s, 0), v)
        if eng in self.SERIAL:
            prev = self.last.get(eng)
            if prev is not None:
                w[prev[0]] = max(w.get(prev[0], 0), prev[1])
        if inc is not None:
            self.c[inc] = self.c.get(inc, 0) + by
            if eng in self.SERIAL:
                self.last[eng] = (inc, self.c[inc])
        self.ops[eng].append((fn, w, None if selfinc else inc, by))
        return self.c.get(inc) if inc is not None else None

    tensor_mark_fn = None

    def mark(self, eng, sem, waits=()):
        if eng == "tensor":
            return self.add(eng, self.tensor_mark_fn, waits=waits, inc=sem)
        return self.add(eng, lambda en, s=sem: en.sem_inc(self._sems[s], 1),
                        waits=waits, inc=sem, selfinc=True)

    def pend(self, eng, sem, v):
        self.pending[eng][sem] = max(self.pending[eng].get(sem, 0), v)

    def barrier(self):
        marks = []
        for e in ENGINES:
            w = []
            if e == "sync":
                w = [("sIN", self.val("sIN"))]
            elif e == "gpsimd":
                w = [("sAUX", self.val("sAUX")), ("sCC", self.val("sCC"))]
            elif e == "tensor":
                w = [("sDVE", self.val("sDVE")), ("sACT", self.val("sACT"))]
            v = self.mark(e, ESEM[e], waits=w)
            marks.append((ESEM[e], v))
        for e in ENGINES:
            for s, v in marks:
                self.pend(e, s, v)

    def emit(self, block, sems):
        self._sems = sems
        def mk(engname):
            def run(eng):
                seen = {}
                for fn, waits, inc, by in self.ops[engname]:
                    for s, v in waits.items():
                        if v > 0 and seen.get(s, -1) < v:
                            eng.wait_ge(sems[s], v)
                            seen[s] = v
                    ins = fn(eng)
                    if inc is not None:
                        ins.then_inc(sems[inc], by)
            return run
        block.sync(mk("sync"))
        block.gpsimd(mk("gpsimd"))
        block.tensor(mk("tensor"))
        block.vector(mk("vector"))
        block.scalar(mk("scalar"))


def build_nc():
    nc = bass.Bass("TRN2", target_bir_lowering=False,
                   detect_race_conditions=RACE_DETECT)

    x_my = nc.dram_tensor("x_my", [TOK, D], F32, kind="ExternalInput")
    wi_my = nc.dram_tensor("wi_my", [D, 1536], F32, kind="ExternalInput")
    bi_my = nc.dram_tensor("bi_my", [1, 1536], F32, kind="ExternalInput")
    wo_cat = nc.dram_tensor("wo_cat", [D, D], F32, kind="ExternalInput")
    bo = nc.dram_tensor("bo", [1, D], F32, kind="ExternalInput")
    fw = nc.dram_tensor("fw", [D, D], F32, kind="ExternalInput")
    fb = nc.dram_tensor("fb", [1, D], F32, kind="ExternalInput")
    idx_ln2 = nc.dram_tensor("idx_ln2", [128, 32], U32, kind="ExternalInput")
    idx_ot = nc.dram_tensor("idx_ot", [128, 8], U32, kind="ExternalInput")
    out_my = nc.dram_tensor("out_my", [TOK, D], F32, kind="ExternalOutput")

    qwT_dram = nc.dram_tensor("qwT_dram", [128, 8192], BF16, kind="Internal")
    ln2_in = nc.dram_tensor("ln2_in", [TOK, D], BF16, kind="Internal")
    ln2_out = nc.dram_tensor("ln2_out", [8 * TOK, D], BF16, kind="Internal")
    ot_in = nc.dram_tensor("ot_in", [1024, TOK], BF16, kind="Internal")
    ot_out = nc.dram_tensor("ot_out", [8192, TOK], BF16, kind="Internal")
    sc_dram = nc.dram_tensor("sc_dram", [1, 512], F32, kind="Internal")
    mx_in = nc.dram_tensor("mx_in", [1, 16], F32, kind="Internal")
    mx_out = nc.dram_tensor("mx_out", [1, 16], F32, kind="Internal")

    import contextlib
    es = contextlib.ExitStack()
    SB = lambda nm, sh, dt: es.enter_context(nc.sbuf_tensor(nm, sh, dt))
    PS = lambda nm, sh, dt: es.enter_context(nc.psum_tensor(nm, sh, dt))

    xs = SB("xs", [128, NT * 1024], F32)
    qt = SB("qt", [128, 16384], BF16)
    kt = SB("kt", [128, 16384], BF16)
    vb = SB("vb", [128, 8 * 32 * 65], BF16)
    big = SB("big", [128, 16384], BF16)
    identf = SB("identf", [128, 128], F32)
    identb = SB("identb", [128, 128], BF16)
    ln2st = SB("ln2st", [128, 2 * 1024], BF16)
    expb = SB("expb", [128, 2 * 512], BF16)
    rb = SB("rb", [128, 512], BF16)
    m_a = SB("m_a", [128, 16], F32)
    v_a = SB("v_a", [128, 16], F32)
    rs_a = SB("rs_a", [128, 16], F32)
    nm_a = SB("nm_a", [128, 16], F32)
    mx_a = SB("mx_a", [128, 16], F32)
    mn_a = SB("mn_a", [128, 16], F32)
    ws_a = SB("ws_a", [128, 16], F32)
    sc = SB("sc", [128, 12], F32)
    c4 = SB("c4", [128, 4], F32)
    bcol = SB("bcol", [128, 8], F32)
    scr = SB("scr", [1, 16], F32)
    msc = SB("msc", [1, 16], F32)
    onef = SB("onef", [1, 128], F32)
    oneb = SB("oneb", [1, 128], BF16)
    onec = SB("onec", [128, 1], F32)
    rowt = SB("rowt", [1, 512], F32)
    rowb = SB("rowb", [1, 1536], BF16)
    biasq = rowb.ap()[0:1, 0:1024]
    birb = rowb.ap()
    bvrow = rowb.ap()[0:1, 1024:1536]
    rD = rowb.ap()[0:1, 0:1024]
    borow = rowb.ap()[0:1, 0:1024]
    idxl = SB("idxl", [128, 32], U32)
    idxo = SB("idxo", [128, 8], U32)

    psA = PS("psA", [128, 1024], F32)
    psC = PS("psC", [128, 512], F32)
    psD = PS("psD", [128, 512], F32)
    psE = PS("psE", [128, 512], F32)
    psF = PS("psF", [128, 512], F32)
    psT = PS("psT", [128, 1024], BF16)

    qtf = qt.ap().bitcast(F32)
    qf32 = [qtf[:, 0:1024], qtf[:, 1024:2048]]
    sqs = qtf[:, 1024:2048]
    fwst = qtf[:, 2048:3072]
    abst = qtf[:, 3072:4096]
    sgst = qtf[:, 4096:5120]
    wist = qtf[:, 5120:6656]
    birow = qtf[0:1, 5120:6656]
    fbrow = qtf[0:1, 6656:7680]
    worow = big.ap().bitcast(F32)[:, 0:1024]
    ktb = kt.ap()
    qb_t = [ktb[:, 0:1024], ktb[:, 1024:2048]]
    qT_t = [ktb[:, 2048:3072], ktb[:, 3072:4096]]

    P = Prog()
    P.tensor_mark_fn = lambda e: e.transpose(psT[:, 0:128], identb[:, :],
                                             identb[:, :])

    def dma(eng_name, dst, src, waits=(), sem=None, slow=False):
        sem = sem or ("sIN" if eng_name == "sync" else "sAUX")
        v = P.add(eng_name, lambda e, d=dst, s=src, sl=slow: e.dma_start(
            out=d, in_=s, allow_slow_non_contiguous=sl), waits=waits,
            inc=sem, by=16)
        P.pend(eng_name, sem, v)
        return v

    def col_reduce_dma(out_cell, op, wait_dve):
        d1 = dma("gpsimd", sc_dram.ap().rearrange("one (p c) -> (one p) c", c=4),
                 c4[:, :], waits=[("sDVE", wait_dve + 0),
                                  ("sAUX", P.val("sAUX")),
                                  ("sDVE", P.val("sDVE"))])
        d2 = dma("gpsimd", rowt[:, :], sc_dram[:, :], waits=[("sAUX", d1)])
        return P.add("vector", lambda e: e.tensor_reduce(
            out_cell, bass.AP(rowt, 0, [[512, 1], [4, 128]]), AXX, op),
            waits=[("sAUX", d2)], inc="sDVE")

    def bcast_dma(cell, slot, wait_dve):
        m = P.add("vector", lambda e: e.memset(rowt[:, :], 1.0),
                  waits=[("sDVE", wait_dve), ("sAUX", P.val("sAUX"))],
                  inc="sDVE")
        r = P.add("vector", lambda e: e.tensor_scalar(
            rowt[:, :], rowt[:, :], cell, None, ALU.mult), inc="sDVE")
        d1 = dma("gpsimd", sc_dram[:, :], rowt[:, :],
                 waits=[("sDVE", r), ("sAUX", P.val("sAUX"))])
        d2 = dma("gpsimd", sc[:, 4 * slot:4 * slot + 4],
                 sc_dram.ap().rearrange("one (p c) -> (one p) c", c=4),
                 waits=[("sAUX", d1)])
        return d2

    for t in range(NT):
        dma("sync", xs[:, 1024 * t:1024 * (t + 1)], x_my[128 * t:128 * (t + 1), :])
    dma("sync", fbrow, fb[:, :])
    dma("sync", idxl[:, :], idx_ln2[:, :])
    dma("sync", idxo[:, :], idx_ot[:, :])

    if STAGE < 0.15:
        for t in range(NT):
            dma("sync", out_my[128 * t:128 * (t + 1), :],
                xs[:, 1024 * t:1024 * (t + 1)],
                waits=[("sIN", P.val("sIN"))])
        P.mark("sync", "sSN", waits=[("sIN", P.val("sIN"))])
        return _finish(nc, P, es)

    P.add("gpsimd", lambda e: e.memset(identf[:, :], 1.0))
    P.add("gpsimd", lambda e: e.affine_select(identf[:, :], identf[:, :],
          [[-1, 128]], ALU.is_equal, 0.0, base=0, channel_multiplier=1))
    P.add("gpsimd", lambda e: e.memset(onef[:, :], 1.0))
    P.add("gpsimd", lambda e: e.memset(oneb[:, :], 1.0))
    P.add("gpsimd", lambda e: e.memset(onec[:, :], 1.0))
    P.add("gpsimd", lambda e: e.memset(msc[:, :], 0.0))
    P.add("gpsimd", lambda e: e.memset(vb[:, :], 1.0), inc="sPOOL")
    pool_const = P.val("sPOOL")
    P.add("vector", lambda e: e.tensor_copy(identb[:, :], identf[:, :]),
          waits=[("sPOOL", pool_const)], inc="sDVE")

    if STAGE < 0.19:
        for t in range(NT):
            dma("sync", out_my[128 * t:128 * (t + 1), :],
                xs[:, 1024 * t:1024 * (t + 1)],
                waits=[("sIN", P.val("sIN")), ("sDVE", P.val("sDVE"))])
        P.mark("sync", "sSN", waits=[("sIN", P.val("sIN"))])
        return _finish(nc, P, es)

    redA = []
    for o in range(8):
        st = fwst if o % 2 == 0 else abst
        w = [("sDVE", redA[o - 2])] if o >= 2 else ()
        d = dma("sync", st, fw[128 * o:128 * (o + 1), :], waits=w)
        r = P.add("vector", lambda e, s=st, o=o: e.reduce_sum(
            ws_a[:, o:o + 1], s, AXX, apply_absolute_value=True),
            waits=[("sIN", d)], inc="sDVE")
        redA.append(r)
    for j in range(4):
        P.add("vector", lambda e, j=j: e.reduce_sum(c4[:, j:j + 1],
              ws_a[:, 0:8], AXX), inc="sDVE")
    col_reduce_dma(scr[0:1, 0:1], ALU.add, P.val("sDVE"))
    P.add("vector", lambda e: e.tensor_scalar(scr[0:1, 0:1], scr[0:1, 0:1],
          1.0 / (1 << 20), None, ALU.mult), inc="sDVE")
    P.add("vector", lambda e: e.tensor_scalar(scr[0:1, 7:8], scr[0:1, 0:1],
          0.5, None, ALU.mult), inc="sDVE")
    hwb = bcast_dma(scr[0:1, 7:8], 0, P.val("sDVE"))
    hw_done = P.add("vector", lambda e: e.tensor_copy(mn_a[:, 1:2], sc[:, 0:1]),
                    waits=[("sAUX", hwb)], inc="sDVE")

    if STAGE < 0.199:
        for t in range(NT):
            dma("sync", out_my[128 * t:128 * (t + 1), :],
                xs[:, 1024 * t:1024 * (t + 1)],
                waits=[("sIN", P.val("sIN")), ("sDVE", P.val("sDVE"))])
        P.mark("sync", "sSN", waits=[("sIN", P.val("sIN"))])
        return _finish(nc, P, es)

    lastB = [0, 0]
    last_psT = 0
    last_tr = 0
    prev_qw = 0
    for o in range(8):
        st = fwst if o % 2 == 0 else abst
        d2 = dma("sync", st, fw[128 * o:128 * (o + 1), :],
                 waits=[("sDVE", max(redA[-1], lastB[o % 2]))])
        if True:
            sg_v = P.add("scalar", lambda e, s=st: e.activation(sgst, s, AF.Sign),
                         waits=[("sIN", d2), ("sDVE", max(hw_done, prev_qw))],
                         inc="sACT")
            ab_v = P.add("scalar", lambda e, s=st: e.activation(qf32[0], s, AF.Abs),
                         inc="sACT")
            g_v = P.add("vector", lambda e: e.tensor_scalar(
                qf32[0], qf32[0], sc[:, 0:1], None, ALU.is_gt),
                waits=[("sACT", ab_v)], inc="sDVE")
            qw_v = P.add("vector", lambda e: e.tensor_mul(qb_t[0], sgst, qf32[0]),
                         waits=[("sACT", sg_v), ("sDVE", g_v),
                                ("sPE", last_tr)], inc="sDVE")
        lastB[o % 2] = qw_v
        prev_qw = qw_v
        if STAGE < 0.21:
            continue
        for g in range(2):
            def trfn(e, g=g):
                for j in range(4):
                    e.transpose(psT[:, 128 * j:128 * (j + 1)],
                                qb_t[0][:, (4 * g + j) * 128:(4 * g + j + 1) * 128],
                                identb[:, :])
                return e.drain()
            tr_v = P.add("tensor", trfn,
                         waits=[("sDVE", max(qw_v, last_psT))], inc="sPE")
            last_tr = tr_v
            if True:
                last_psT = P.add("vector", lambda e, g=g, o=o: e.tensor_copy(
                    bass.AP(big, (4 * g) * 1024 + o * 128,
                            [[16384, 128], [1024, 4], [1, 128]]),
                    psT.ap()[:, 0:512].rearrange("p (j c) -> p j c", j=4)),
                    waits=[("sPE", tr_v)], inc="sDVE")

    def ln_stats(minmax):
        for t in range(NT):
            xt = xs[:, 1024 * t:1024 * (t + 1)]
            P.add("vector", lambda e, xt=xt, t=t: e.reduce_sum(
                m_a[:, t:t + 1], xt, AXX), inc="sDVE")
            P.add("vector", lambda e, xt=xt: e.tensor_mul(sqs, xt, xt),
                  inc="sDVE")
            P.add("vector", lambda e, t=t: e.reduce_sum(
                v_a[:, t:t + 1], sqs, AXX), inc="sDVE")
            if minmax:
                P.add("vector", lambda e, xt=xt, t=t: e.reduce_max(
                    mx_a[:, t:t + 1], xt, AXX), inc="sDVE")
                P.add("vector", lambda e, xt=xt, t=t: e.tensor_reduce(
                    mn_a[:, t:t + 1], xt, AXX, ALU.min), inc="sDVE")
        P.add("vector", lambda e: e.tensor_scalar(m_a[:, :], m_a[:, :],
              1.0 / D, None, ALU.mult), inc="sDVE")
        P.add("vector", lambda e: e.tensor_scalar(v_a[:, :], v_a[:, :],
              1.0 / D, None, ALU.mult), inc="sDVE")
        P.add("vector", lambda e: e.tensor_mul(ws_a[:, :], m_a[:, :], m_a[:, :]),
              inc="sDVE")
        P.add("vector", lambda e: e.tensor_sub(v_a[:, :], v_a[:, :], ws_a[:, :]),
              inc="sDVE")
        P.add("vector", lambda e: e.tensor_scalar(v_a[:, :], v_a[:, :],
              EPS, None, ALU.add), inc="sDVE")
        vd = P.val("sDVE")
        P.add("scalar", lambda e: e.activation(ws_a[:, :], v_a[:, :], AF.Ln),
              waits=[("sDVE", vd)], inc="sACT")
        P.add("scalar", lambda e: e.activation(rs_a[:, :], ws_a[:, :], AF.Exp,
              scale=-0.5), inc="sACT")
        return P.val("sACT")

    def iscale_collective():
        rsd = P.val("sACT")
        P.add("vector", lambda e: e.tensor_sub(mx_a[:, :], mx_a[:, :], m_a[:, :]),
              waits=[("sACT", rsd)], inc="sDVE")
        P.add("vector", lambda e: e.tensor_sub(mn_a[:, :], m_a[:, :], mn_a[:, :]),
              inc="sDVE")
        P.add("vector", lambda e: e.tensor_max(mx_a[:, :], mx_a[:, :], mn_a[:, :]),
              inc="sDVE")
        P.add("vector", lambda e: e.tensor_mul(mx_a[:, :], mx_a[:, :], rs_a[:, :]),
              inc="sDVE")
        for j in range(4):
            P.add("vector", lambda e, j=j: e.reduce_max(c4[:, j:j + 1],
                  mx_a[:, :], AXX), inc="sDVE")
        col_reduce_dma(scr[0:1, 1:2], ALU.max, P.val("sDVE"))
        lm = P.add("vector", lambda e: e.tensor_copy(msc[0:1, 0:1],
                   scr[0:1, 1:2]), inc="sDVE")
        d = dma("gpsimd", mx_in[:, :], msc[:, :], waits=[("sDVE", lm)])
        P.add("gpsimd", lambda e: e.collective_compute(
            "AllReduce", ALU.max, replica_groups=[list(range(8))],
            ins=[mx_in.ap().opt()], outs=[mx_out.ap().opt()]),
            waits=[("sAUX", d)], inc="sCC")
        gl = dma("gpsimd", msc[:, :], mx_out[:, :],
                 waits=[("sCC", P.val("sCC"))])
        P.add("vector", lambda e: e.tensor_scalar(scr[0:1, 2:3], msc[0:1, 0:1],
              1.0 / 127.0, None, ALU.mult), waits=[("sAUX", gl)], inc="sDVE")
        P.add("vector", lambda e: e.reciprocal(scr[0:1, 3:4], scr[0:1, 2:3]),
              inc="sDVE")
        P.add("vector", lambda e: e.tensor_mul(scr[0:1, 4:5], scr[0:1, 2:3],
              scr[0:1, 0:1]), inc="sDVE")
        P.add("vector", lambda e: e.tensor_scalar(scr[0:1, 5:6], scr[0:1, 4:5],
              0.5, None, ALU.mult), inc="sDVE")
        P.add("vector", lambda e: e.reciprocal(scr[0:1, 6:7], scr[0:1, 4:5]),
              inc="sDVE")
        P.add("vector", lambda e: e.tensor_scalar(biasq[:, :], fbrow,
              scr[0:1, 6:7], None, ALU.mult), inc="sDVE")
        sd = P.val("sDVE")
        bcast_dma(scr[0:1, 3:4], 1, sd)
        b2 = bcast_dma(scr[0:1, 5:6], 2, sd)
        P.mark("vector", "sDVE", waits=[("sAUX", b2)])
        P.add("vector", lambda e: e.tensor_scalar(rs_a[:, :], rs_a[:, :],
              sc[:, 4:5], None, ALU.mult), inc="sDVE")
        P.add("vector", lambda e: e.tensor_mul(nm_a[:, :], m_a[:, :],
              rs_a[:, :]), inc="sDVE")
        P.add("vector", lambda e: e.tensor_scalar(nm_a[:, :], nm_a[:, :],
              -1.0, None, ALU.mult), inc="sDVE")

    def ffn_pass():
        prev_stt = P.val("sDVE")
        rv_hist = [0, 0]
        psT_ev = P.val("sDVE")
        for t in range(NT):
            par = t % 2
            xt = xs[:, 1024 * t:1024 * (t + 1)]
            if True:
                qv = P.add("scalar", lambda e, xt=xt, t=t, par=par: e.activation(
                    qf32[par], xt, AF.Identity, bias=nm_a[:, t:t + 1],
                    scale=rs_a[:, t:t + 1]), waits=[("sDVE", rv_hist[par])],
                    inc="sACT")
                rv = P.add("vector", lambda e, par=par: e.tensor_scalar(
                    qb_t[par], qf32[par], MAGIC, MAGIC, ALU.add, ALU.subtract),
                    waits=[("sACT", qv)], inc="sDVE")
            rv_hist[par] = rv
            evs = []
            for g in range(2):
                tv = P.add("tensor", lambda e, g=g, par=par: [
                    e.transpose(psT[:, 128 * j:128 * (j + 1)],
                                qb_t[par][:, (4 * g + j) * 128:(4 * g + j + 1) * 128],
                                identb[:, :]) for j in range(4)][-1],
                    waits=[("sDVE", max(rv, psT_ev))], inc="sPE")
                psT_ev = P.add("vector", lambda e, g=g, par=par: e.tensor_copy(
                    qT_t[par][:, 512 * g:512 * (g + 1)], psT[:, 0:512]),
                    waits=[("sPE", tv)], inc="sDVE")
                evs.append(psT_ev)
            stts = [prev_stt]
            for h2 in range(2):
                ps = psC if h2 == 0 else psD
                mm = P.add("tensor", lambda e, par=par, h2=h2, ps=ps: [
                    e.matmul(ps[:, :],
                             qT_t[par][:, 128 * k:128 * (k + 1)],
                             big[:, 1024 * k + 512 * h2:
                                 1024 * k + 512 * (h2 + 1)],
                             start=(k == 0), stop=False) for k in range(8)][-1],
                    waits=[("sDVE", max(evs + [prev_stt]))], inc="sPE")
                mm = P.add("tensor", lambda e, h2=h2, ps=ps: e.matmul(
                    ps[:, :], oneb[:, :], biasq[:, 512 * h2:512 * (h2 + 1)],
                    start=False, stop=True), inc="sPE")
                stts.append(P.add("vector", lambda e, xt=xt, h2=h2, ps=ps:
                    e.scalar_tensor_tensor(
                        xt[:, 512 * h2:512 * (h2 + 1)], ps[:, :], sc[:, 8:9],
                        xt[:, 512 * h2:512 * (h2 + 1)], ALU.mult, ALU.add),
                    waits=[("sPE", mm)], inc="sDVE"))
            prev_stt = max(stts)

    P.barrier()
    if STAGE >= 0.3:
        ln_stats(minmax=True)
        iscale_collective()
        P.barrier()
    if STAGE >= 1:
        ffn_pass()
        P.barrier()

    if STAGE >= 2:
        for k in range(8):
            dma("sync", qwT_dram[:, 1024 * k:1024 * (k + 1)],
                big[:, 1024 * k:1024 * (k + 1)])
        qwT_saved = P.val("sIN")
        ln_stats(minmax=False)
        rsd = P.val("sACT")
        P.add("vector", lambda e: e.tensor_mul(nm_a[:, :], m_a[:, :],
              rs_a[:, :]), waits=[("sACT", rsd)], inc="sDVE")
        st_done = P.add("vector", lambda e: e.tensor_scalar(nm_a[:, :],
                        nm_a[:, :], -1.0, None, ALU.mult), inc="sDVE")
        ln2dma = []
        for t in range(NT):
            par = t % 2
            xt = xs[:, 1024 * t:1024 * (t + 1)]
            w = [("sDVE", st_done)]
            if t >= 2:
                w.append(("sIN", ln2dma[t - 2]))
            av = P.add("scalar", lambda e, xt=xt, t=t, par=par: e.activation(
                ln2st[:, 1024 * par:1024 * (par + 1)], xt, AF.Identity,
                bias=nm_a[:, t:t + 1], scale=rs_a[:, t:t + 1]),
                waits=w, inc="sACT")
            d = dma("sync", ln2_in[128 * t:128 * (t + 1), :],
                    ln2st[:, 1024 * par:1024 * (par + 1)],
                    waits=[("sACT", av)])
            ln2dma.append(d)
        P.add("gpsimd", lambda e: e.collective_compute(
            "AllGather", ALU.bypass, replica_groups=[list(range(8))],
            ins=[ln2_in.ap().opt()], outs=[ln2_out.ap().opt()]),
            waits=[("sIN", ln2dma[-1]), ("sAUX", P.val("sAUX"))], inc="sCC")
        ag1 = P.val("sCC")
        wic = 0
        for f in range(8):
            d = dma("sync", wist, wi_my[128 * f:128 * (f + 1), :],
                    waits=[("sIN", qwT_saved), ("sDVE", wic)])
            wic = P.add("vector", lambda e, f=f: e.tensor_copy(
                big[:, 1536 * f:1536 * (f + 1)], wist),
                waits=[("sIN", d)], inc="sDVE")
        bd_ = dma("sync", birow, bi_my[:, :], waits=[("sDVE", wic)])
        bcv = P.add("vector", lambda e: e.tensor_copy(birb, birow),
                    waits=[("sIN", bd_)], inc="sDVE")
        btr = P.add("tensor", lambda e: [
            e.transpose(psT[0:128, 2 * j:2 * j + 1],
                        birb[:, 128 * j:128 * (j + 1)],
                        identb[0:1, 0:1]) for j in range(8)][-1],
            waits=[("sDVE", bcv)], inc="sPE")
        P.add("vector", lambda e: e.tensor_copy(
            bcol[:, :], psT.ap()[:, 0:16].rearrange("p (j two) -> p j two",
                                                    two=2)[:, :, 0]),
            waits=[("sPE", btr)], inc="sDVE")
        P.add("vector", lambda e: e.tensor_scalar(bcol[:, 0:4], bcol[:, 0:4],
              0.125, None, ALU.mult), inc="sDVE")
        P.barrier()

        glob_tr = []
        for bd in range(NB):
            for tt in range(4):
                par = tt % 2
                g = 4 * bd + tt
                w = [("sCC", ag1)]
                if g >= 2:
                    w.append(("sPE", glob_tr[g - 2]))
                d = P.add("gpsimd", lambda e, g=g, par=par: e.indirect_dma_start(
                    ln2st[:, 1024 * par:1024 * (par + 1)], None,
                    ln2_out.ap(), bass.IndirectOffsetOnAxis(idxl[:, g:g + 1], 0)),
                    waits=w, inc="sAUX", by=16)
                for gg in range(2):
                    tv = P.add("tensor", lambda e, gg=gg, par=par: [
                        e.transpose(psT[:, 128 * j:128 * (j + 1)],
                                    ln2st[:, 1024 * par + (4 * gg + j) * 128:
                                          1024 * par + (4 * gg + j + 1) * 128],
                                    identb[:, :]) for j in range(4)][-1],
                        waits=[("sAUX", d), ("sDVE", P.val("sDVE"))], inc="sPE")
                    P.add("vector", lambda e, gg=gg, tt=tt: e.tensor_copy(
                        bass.AP(big, 12288 + (4 * gg) * 512 + 128 * tt,
                                [[16384, 128], [512, 4], [1, 128]]),
                        psT.ap()[:, 0:512].rearrange("p (j c) -> p j c", j=4)),
                        waits=[("sPE", tv)], inc="sDVE")
                glob_tr.append(P.val("sPE"))
            lnt_done = P.val("sDVE")
            for oc in range(8):
                ps = psC if oc % 2 == 0 else psD
                mm = P.add("tensor", lambda e, oc=oc, ps=ps: [
                    e.matmul(ps[:, :],
                             big[:, 1536 * f + 128 * oc:1536 * f + 128 * (oc + 1)],
                             big[:, 12288 + 512 * f:12288 + 512 * (f + 1)],
                             start=(f == 0), stop=(f == 7)) for f in range(8)][-1],
                    waits=[("sDVE", lnt_done), ("sACT", P.val("sACT"))],
                    inc="sPE")
                isq = oc < 4
                dest = (qt if isq else kt)
                doff = (oc if isq else oc - 4) * 4096 + 512 * bd
                P.add("scalar", lambda e, ps=ps, dest=dest, doff=doff, oc=oc,
                      isq=isq: e.activation(
                          dest[:, doff:doff + 512], ps[:, :], AF.Identity,
                          bias=bcol[:, oc:oc + 1], scale=(0.125 if isq else 1.0)),
                      waits=[("sPE", mm)], inc="sACT")
            for tt in range(4):
                ps = psE if tt % 2 == 0 else psF
                kc = 4 * bd + tt
                mm = P.add("tensor", lambda e, tt=tt, ps=ps: [
                    e.matmul(ps[:, :],
                             big[:, 12288 + 512 * f + 128 * tt:
                                  12288 + 512 * f + 128 * (tt + 1)],
                             big[:, 1536 * f + 1024:1536 * (f + 1)],
                             start=(f == 0), stop=False) for f in range(8)][-1],
                    waits=[("sDVE", P.val("sDVE"))], inc="sPE")
                mm = P.add("tensor", lambda e, ps=ps: e.matmul(
                    ps[:, :], oneb[:, :], bvrow, start=False, stop=True),
                    inc="sPE")
                P.add("vector", lambda e, ps=ps, kc=kc: e.tensor_copy(
                    bass.AP(vb, 65 * kc, [[8 * 32 * 65, 128], [2080, 8], [1, 64]]),
                    ps.ap().rearrange("p (j c) -> p j c", j=8)),
                    waits=[("sPE", mm)], inc="sDVE")
        P.barrier()

        for pp in range(4):
            for bd in range(NB):
                kmax = 4 * bd + 4
                qbase = 4096 * pp + 512 * bd
                exp_done = [0] * (kmax + 1)
                pool_done = [0] * (kmax + 1)
                av_done = [0, 0]
                for kc in range(kmax):
                    j0 = max(0, kc - 4 * bd)
                    qo = 128 * j0
                    par = kc % 2
                    eA = expb[:, 0:512]
                    eB = expb[:, 512:1024]
                    w = [("sACT", exp_done[kc - 1] if kc >= 1 else 0),
                         ("sPE", av_done[0])]
                    sA = P.add("tensor", lambda e, pp=pp, kc=kc, qo=qo,
                               qbase=qbase: e.matmul(
                                   psC[:, qo:512],
                                   kt[0:64, 4096 * pp + 128 * kc:
                                      4096 * pp + 128 * (kc + 1)],
                                   qt[0:64, qbase + qo:qbase + 512],
                                   start=True, stop=True), waits=w, inc="sPE")
                    sB = P.add("tensor", lambda e, pp=pp, kc=kc, qo=qo,
                               qbase=qbase: e.matmul(
                                   psD[:, qo:512],
                                   kt[64:128, 4096 * pp + 128 * kc:
                                      4096 * pp + 128 * (kc + 1)],
                                   qt[64:128, qbase + qo:qbase + 512],
                                   start=True, stop=True,
                                   tile_position=(64, 0)), inc="sPE")
                    we = [("sPE", sA)]
                    if kc >= 2:
                        we.append(("sPOOL", pool_done[kc - 2]))
                    P.add("scalar", lambda e, qo=qo, eA=eA: e.activation(
                        eA[:, qo:512], psC[:, qo:512], AF.Exp),
                        waits=we, inc="sACT")
                    ed = P.add("scalar", lambda e, qo=qo, eB=eB: e.activation(
                        eB[:, qo:512], psD[:, qo:512], AF.Exp),
                        waits=[("sPE", sB)], inc="sACT")
                    exp_done[kc] = ed
                    wv = [("sACT", ed)]
                    if kc >= 4 * bd:
                        P.add("gpsimd", lambda e, eA=eA, qo=qo: e.affine_select(
                            eA[:, qo:qo + 128], eA[:, qo:qo + 128], [[1, 128]],
                            ALU.is_ge, 0.0, base=0, channel_multiplier=-1),
                            waits=[("sACT", ed)], inc="sPOOL")
                        pd = P.add("gpsimd", lambda e, eB=eB, qo=qo:
                                   e.affine_select(
                                       eB[:, qo:qo + 128], eB[:, qo:qo + 128],
                                       [[1, 128]], ALU.is_ge, 0.0, base=0,
                                       channel_multiplier=-1), inc="sPOOL")
                        pool_done[kc] = pd
                        wv.append(("sPOOL", pd))
                    else:
                        pool_done[kc] = pool_done[kc - 1] if kc else 0
                    P.add("tensor", lambda e, pp=pp, kc=kc, qo=qo, eA=eA,
                          kmax=kmax: e.matmul(
                              psE[0:65, qo:512],
                              vb[:, 2080 * (2 * pp) + 65 * kc:
                                 2080 * (2 * pp) + 65 * kc + 65],
                              eA[:, qo:512],
                              start=(kc == 0), stop=(kc == kmax - 1)),
                          waits=wv, inc="sPE")
                    av_done[par] = P.add("tensor", lambda e, pp=pp, kc=kc,
                        qo=qo, eB=eB, kmax=kmax: e.matmul(
                            psF[0:65, qo:512],
                            vb[:, 2080 * (2 * pp + 1) + 65 * kc:
                               2080 * (2 * pp + 1) + 65 * kc + 65],
                            eB[:, qo:512],
                            start=(kc == 0), stop=(kc == kmax - 1)),
                        inc="sPE")
                last = P.val("sPE")

                def lp(fn):
                    def g(e):
                        with nc.allow_low_precision("softmax denom in bf16"):
                            return fn(e)
                    return g
                P.add("vector", lp(lambda e: e.reciprocal(rD[0:1, 0:512],
                      psE[64:65, :])), waits=[("sPE", last)], inc="sDVE")
                r2 = P.add("vector", lp(lambda e: e.reciprocal(
                    rD[0:1, 512:1024], psF[64:65, :])), inc="sDVE")
                P.add("tensor", lambda e: e.matmul(
                    psC[0:64, :], oneb[0:1, 0:64], rD[0:1, 0:512],
                    start=True, stop=True),
                    waits=[("sDVE", r2), ("sACT", P.val("sACT"))], inc="sPE")
                b2 = P.add("tensor", lambda e: e.matmul(
                    psC[64:128, :], oneb[0:1, 0:64], rD[0:1, 512:1024],
                    start=True, stop=True, tile_position=(0, 64)), inc="sPE")
                rbv = P.add("vector", lambda e: e.tensor_copy(rb[:, :],
                            psC[:, :]), waits=[("sPE", b2)], inc="sDVE")
                P.add("vector", lambda e, qbase=qbase: e.tensor_mul(
                    big[0:64, qbase:qbase + 512], psE[0:64, :], rb[0:64, :]),
                    waits=[("sDVE", rbv)], inc="sDVE")
                n2 = P.add("vector", lambda e, qbase=qbase: e.tensor_mul(
                    big[64:128, qbase:qbase + 512], psF[0:64, :],
                    rb[64:128, :]), inc="sDVE")
                P.pend("tensor", "sDVE", n2)
        P.barrier()

        for p4 in range(4):
            for hh in range(2):
                dma("sync", ot_in[128 * (4 * hh + p4):128 * (4 * hh + p4 + 1), :],
                    big[:, 4096 * p4 + 2048 * hh:4096 * p4 + 2048 * (hh + 1)])
        otd = P.val("sIN")
        P.add("gpsimd", lambda e: e.collective_compute(
            "AllGather", ALU.bypass, replica_groups=[list(range(8))],
            ins=[ot_in.ap().opt()], outs=[ot_out.ap().opt()]),
            waits=[("sIN", otd), ("sAUX", P.val("sAUX"))], inc="sCC")
        ag2 = P.val("sCC")
        got = 0
        for g in range(8):
            dest = kt if g < 4 else qt
            doff = 2048 * (g % 4)
            got = P.add("gpsimd", lambda e, g=g, dest=dest, doff=doff:
                        e.indirect_dma_start(
                            dest[:, doff:doff + 2048], None, ot_out.ap(),
                            bass.IndirectOffsetOnAxis(idxo[:, g:g + 1], 0)),
                        waits=[("sCC", ag2)], inc="sAUX", by=16)
        woc = 0
        for j in range(8):
            d = dma("sync", worow, wo_cat[128 * j:128 * (j + 1), :],
                    waits=[("sIN", otd), ("sDVE", woc)])
            woc = P.add("vector", lambda e, j=j: e.tensor_copy(
                qt[:, 8192 + 1024 * j:8192 + 1024 * (j + 1)], worow),
                waits=[("sIN", d)], inc="sDVE")
        bod = dma("sync", worow[0:1, :], bo[:, :], waits=[("sDVE", woc)])
        bocast = P.add("vector", lambda e: e.tensor_copy(borow[:, :],
                       worow[0:1, :]), waits=[("sIN", bod)], inc="sDVE")
        P.barrier()

        prev = max(P.val("sDVE"), bocast)
        for t in range(NT):
            mm = P.add("tensor", lambda e, t=t: [
                e.matmul(psA[:, 512 * h2:512 * (h2 + 1)],
                         (kt if j < 4 else qt)[:, 2048 * (j % 4) + 128 * t:
                                               2048 * (j % 4) + 128 * (t + 1)],
                         qt[:, 8192 + 1024 * j + 512 * h2:
                            8192 + 1024 * j + 512 * (h2 + 1)],
                         start=(j == 0), stop=False)
                for j in range(8) for h2 in range(2)][-1],
                waits=[("sAUX", got), ("sDVE", prev)], inc="sPE")
            mm = P.add("tensor", lambda e: [
                e.matmul(psA[:, 512 * h2:512 * (h2 + 1)], oneb[:, :],
                         borow[:, 512 * h2:512 * (h2 + 1)],
                         start=False, stop=True) for h2 in range(2)][-1],
                inc="sPE")
            prev = P.add("vector", lambda e, t=t: [
                e.tensor_add(
                    xs[:, 1024 * t + 512 * h2:1024 * t + 512 * (h2 + 1)],
                    xs[:, 1024 * t + 512 * h2:1024 * t + 512 * (h2 + 1)],
                    psA[:, 512 * h2:512 * (h2 + 1)]) for h2 in range(2)][-1],
                waits=[("sPE", mm)], inc="sDVE")
        P.barrier()

    if STAGE >= 3:
        for k in range(8):
            dma("sync", big[:, 1024 * k:1024 * (k + 1)],
                qwT_dram[:, 1024 * k:1024 * (k + 1)])
        qw2 = dma("sync", fbrow, fb[:, :])
        ln_stats(minmax=True)
        iscale_collective()
        P.barrier()
        P.pend("tensor", "sIN", qw2)
        ffn_pass()
        P.barrier()

    for t in range(NT):
        dma("sync", out_my[128 * t:128 * (t + 1), :],
            xs[:, 1024 * t:1024 * (t + 1)])
    P.mark("sync", "sSN", waits=[("sIN", P.val("sIN"))])

    return _finish(nc, P, es)


def _finish(nc, P, es):
    with (
        nc.semaphore("sIN") as sIN, nc.semaphore("sAUX") as sAUX,
        nc.semaphore("sCC") as sCC, nc.semaphore("sPE") as sPE,
        nc.semaphore("sDVE") as sDVE, nc.semaphore("sACT") as sACT,
        nc.semaphore("sPOOL") as sPOOL, nc.semaphore("sSN") as sSN,
        nc.semaphore("sGN") as sGN,
        nc.Block() as block,
    ):
        sems = dict(sIN=sIN, sAUX=sAUX, sCC=sCC, sPE=sPE, sDVE=sDVE,
                    sACT=sACT, sPOOL=sPOOL, sSN=sSN, sGN=sGN)
        P.emit(block, sems)
    es.close()
    return nc


_NC_CACHE = {}


def kernel(L0_attn_x, L0_attn_w_in, L0_attn_b_in, L0_attn_w_out, L0_attn_b_out,
           L0_ffn_weight, L0_ffn_bias):
    x = np.ascontiguousarray(np.asarray(L0_attn_x, dtype=np.float32))
    w_in = np.ascontiguousarray(np.asarray(L0_attn_w_in, dtype=np.float32))
    b_in = np.asarray(L0_attn_b_in, dtype=np.float32).reshape(1, 3072)
    w_out = np.ascontiguousarray(np.asarray(L0_attn_w_out, dtype=np.float32))
    b_out = np.asarray(L0_attn_b_out, dtype=np.float32).reshape(1, D)
    fw = np.ascontiguousarray(np.asarray(L0_ffn_weight, dtype=np.float32))
    fb = np.asarray(L0_ffn_bias, dtype=np.float32).reshape(1, D)

    in_maps = []
    for c in range(8):
        b, h = c // 2, c % 2
        qs = slice(512 * h, 512 * h + 512)
        ks = slice(1024 + 512 * h, 1024 + 512 * h + 512)
        vs = slice(2048 + 512 * h, 2048 + 512 * h + 512)
        wi = np.ascontiguousarray(
            np.concatenate([w_in[:, qs], w_in[:, ks], w_in[:, vs]], axis=1))
        bi = np.ascontiguousarray(
            np.concatenate([b_in[:, qs], b_in[:, ks], b_in[:, vs]], axis=1))
        wo = np.ascontiguousarray(np.concatenate(
            [w_out[512 * h:512 * h + 512, :],
             w_out[512 * (1 - h):512 * (1 - h) + 512, :]], axis=0))
        idxl_np = (b * 4096 + np.arange(4096, dtype=np.uint32)
                   ).reshape(32, 128).T.copy()
        rows = np.concatenate([
            c * 1024 + 512 * h + np.arange(512, dtype=np.uint32),
            (c ^ 1) * 1024 + 512 * h + np.arange(512, dtype=np.uint32)])
        idxo_np = rows.reshape(8, 128).T.copy()
        in_maps.append(dict(
            x_my=np.ascontiguousarray(x[b, 2048 * h:2048 * h + 2048, :]),
            wi_my=wi, bi_my=bi, wo_cat=wo, bo=b_out,
            fw=fw, fb=fb, idx_ln2=idxl_np, idx_ot=idxo_np))

    if "nc" not in _NC_CACHE:
        _NC_CACHE["nc"] = build_nc()
    res = run_bass_kernel_spmd(_NC_CACHE["nc"], in_maps, core_ids=list(range(8)))
    out = np.empty((B, S, D), dtype=np.float32)
    for c in range(8):
        b, h = c // 2, c % 2
        out[b, 2048 * h:2048 * h + 2048, :] = res.results[c]["out_my"]
    return out


# version 05
# speedup vs baseline: 1.0143x; 1.0143x over previous
import numpy as np
import ml_dtypes
import concourse.bass as bass
import concourse.mybir as mybir
from concourse.bass_utils import run_bass_kernel_spmd

F32 = mybir.dt.float32
BF16 = mybir.dt.bfloat16
U32 = mybir.dt.uint32
AF = mybir.ActivationFunctionType
ALU = mybir.AluOpType
AXX = mybir.AxisListType.X

B, S, D = 4, 4096, 1024
TOK = 2048
NT = TOK // 128
NB = S // 512
MAGIC = 1.5 * 2.0 ** 23
EPS = 1e-5
STAGE = 3
RACE_DETECT = False

ENGINES = ("sync", "gpsimd", "tensor", "vector", "scalar")
ESEM = {"sync": "sSN", "gpsimd": "sGN", "tensor": "sPE",
        "vector": "sDVE", "scalar": "sACT"}


class Prog:

    def __init__(self):
        self.ops = {e: [] for e in ENGINES}
        self.c = {}
        self.pending = {e: {} for e in ENGINES}
        self.last = {}

    def val(self, sem):
        return self.c.get(sem, 0)

    SERIAL = ("vector", "scalar")

    def add(self, eng, fn, waits=(), inc=None, by=1, selfinc=False):
        w = dict(self.pending[eng])
        self.pending[eng] = {}
        for s, v in waits:
            w[s] = max(w.get(s, 0), v)
        if eng in self.SERIAL:
            prev = self.last.get(eng)
            if prev is not None:
                w[prev[0]] = max(w.get(prev[0], 0), prev[1])
        if inc is not None:
            self.c[inc] = self.c.get(inc, 0) + by
            if eng in self.SERIAL:
                self.last[eng] = (inc, self.c[inc])
        self.ops[eng].append((fn, w, None if selfinc else inc, by))
        return self.c.get(inc) if inc is not None else None

    tensor_mark_fn = None

    def mark(self, eng, sem, waits=()):
        if eng == "tensor":
            return self.add(eng, self.tensor_mark_fn, waits=waits, inc=sem)
        return self.add(eng, lambda en, s=sem: en.sem_inc(self._sems[s], 1),
                        waits=waits, inc=sem, selfinc=True)

    def pend(self, eng, sem, v):
        self.pending[eng][sem] = max(self.pending[eng].get(sem, 0), v)

    def barrier(self):
        marks = []
        for e in ENGINES:
            w = []
            if e == "sync":
                w = [("sIN", self.val("sIN"))]
            elif e == "gpsimd":
                w = [("sAUX", self.val("sAUX")), ("sCC", self.val("sCC"))]
            elif e == "tensor":
                w = [("sDVE", self.val("sDVE")), ("sACT", self.val("sACT"))]
            v = self.mark(e, ESEM[e], waits=w)
            marks.append((ESEM[e], v))
        for e in ENGINES:
            for s, v in marks:
                self.pend(e, s, v)

    def emit(self, block, sems):
        self._sems = sems
        def mk(engname):
            def run(eng):
                seen = {}
                for fn, waits, inc, by in self.ops[engname]:
                    for s, v in waits.items():
                        if v > 0 and seen.get(s, -1) < v:
                            eng.wait_ge(sems[s], v)
                            seen[s] = v
                    ins = fn(eng)
                    if inc is not None:
                        ins.then_inc(sems[inc], by)
            return run
        block.sync(mk("sync"))
        block.gpsimd(mk("gpsimd"))
        block.tensor(mk("tensor"))
        block.vector(mk("vector"))
        block.scalar(mk("scalar"))


def build_nc():
    nc = bass.Bass("TRN2", target_bir_lowering=False,
                   detect_race_conditions=RACE_DETECT)

    x_my = nc.dram_tensor("x_my", [TOK, D], F32, kind="ExternalInput")
    wi_my = nc.dram_tensor("wi_my", [D, 1536], F32, kind="ExternalInput")
    bi_my = nc.dram_tensor("bi_my", [1, 1536], F32, kind="ExternalInput")
    wo_cat = nc.dram_tensor("wo_cat", [D, D], F32, kind="ExternalInput")
    bo = nc.dram_tensor("bo", [1, D], F32, kind="ExternalInput")
    fw = nc.dram_tensor("fw", [D, D], F32, kind="ExternalInput")
    fb = nc.dram_tensor("fb", [1, D], F32, kind="ExternalInput")
    idx_ln2 = nc.dram_tensor("idx_ln2", [128, 32], U32, kind="ExternalInput")
    idx_ot = nc.dram_tensor("idx_ot", [128, 8], U32, kind="ExternalInput")
    out_my = nc.dram_tensor("out_my", [TOK, D], F32, kind="ExternalOutput")

    qwT_dram = nc.dram_tensor("qwT_dram", [128, 8192], BF16, kind="Internal")
    ln2_in = nc.dram_tensor("ln2_in", [TOK, D], BF16, kind="Internal")
    ln2_out = nc.dram_tensor("ln2_out", [8 * TOK, D], BF16, kind="Internal")
    ot_in = nc.dram_tensor("ot_in", [1024, TOK], BF16, kind="Internal")
    ot_out = nc.dram_tensor("ot_out", [8192, TOK], BF16, kind="Internal")
    sc_dram = nc.dram_tensor("sc_dram", [1, 512], F32, kind="Internal")
    mx_in = nc.dram_tensor("mx_in", [1, 16], F32, kind="Internal")
    mx_out = nc.dram_tensor("mx_out", [1, 16], F32, kind="Internal")

    import contextlib
    es = contextlib.ExitStack()
    SB = lambda nm, sh, dt: es.enter_context(nc.sbuf_tensor(nm, sh, dt))
    PS = lambda nm, sh, dt: es.enter_context(nc.psum_tensor(nm, sh, dt))

    xs = SB("xs", [128, NT * 1024], F32)
    qt = SB("qt", [128, 16384], BF16)
    kt = SB("kt", [128, 16384], BF16)
    vb = SB("vb", [128, 8 * 32 * 65], BF16)
    big = SB("big", [128, 16384], BF16)
    identf = SB("identf", [128, 128], F32)
    identb = SB("identb", [128, 128], BF16)
    ln2st = SB("ln2st", [128, 2 * 1024], BF16)
    expb = SB("expb", [128, 2 * 512], BF16)
    rb = SB("rb", [128, 512], BF16)
    m_a = SB("m_a", [128, 16], F32)
    v_a = SB("v_a", [128, 16], F32)
    rs_a = SB("rs_a", [128, 16], F32)
    nm_a = SB("nm_a", [128, 16], F32)
    mx_a = SB("mx_a", [128, 16], F32)
    mn_a = SB("mn_a", [128, 16], F32)
    ws_a = SB("ws_a", [128, 16], F32)
    sc = SB("sc", [128, 12], F32)
    c4 = SB("c4", [128, 4], F32)
    bcol = SB("bcol", [128, 8], F32)
    scr = SB("scr", [1, 16], F32)
    msc = SB("msc", [1, 16], F32)
    onef = SB("onef", [1, 128], F32)
    oneb = SB("oneb", [1, 128], BF16)
    onec = SB("onec", [128, 1], F32)
    rowt = SB("rowt", [1, 512], F32)
    rowb = SB("rowb", [1, 1536], BF16)
    biasq = rowb.ap()[0:1, 0:1024]
    birb = rowb.ap()
    bvrow = rowb.ap()[0:1, 1024:1536]
    rD = rowb.ap()[0:1, 0:1024]
    borow = rowb.ap()[0:1, 0:1024]
    idxl = SB("idxl", [128, 32], U32)
    idxo = SB("idxo", [128, 8], U32)

    psA = PS("psA", [128, 1024], F32)
    psC = PS("psC", [128, 512], F32)
    psD = PS("psD", [128, 512], F32)
    psE = PS("psE", [128, 512], F32)
    psF = PS("psF", [128, 512], F32)
    psT = PS("psT", [128, 1024], BF16)

    qtf = qt.ap().bitcast(F32)
    qf32 = [qtf[:, 0:1024], qtf[:, 1024:2048]]
    sqs = qtf[:, 1024:2048]
    fwst = qtf[:, 2048:3072]
    abst = qtf[:, 3072:4096]
    sgst = qtf[:, 4096:5120]
    wist = qtf[:, 5120:6656]
    birow = qtf[0:1, 5120:6656]
    fbrow = qtf[0:1, 6656:7680]
    worow = big.ap().bitcast(F32)[:, 0:1024]
    ktb = kt.ap()
    qb_t = [ktb[:, 0:1024], ktb[:, 1024:2048]]
    qT_t = [ktb[:, 2048:3072], ktb[:, 3072:4096]]

    P = Prog()
    P.tensor_mark_fn = lambda e: e.transpose(psT[:, 0:128], identb[:, :],
                                             identb[:, :])

    def dma(eng_name, dst, src, waits=(), sem=None, slow=False, fence=True):
        sem = sem or ("sIN" if eng_name == "sync" else "sAUX")
        v = P.add(eng_name, lambda e, d=dst, s=src, sl=slow: e.dma_start(
            out=d, in_=s, allow_slow_non_contiguous=sl), waits=waits,
            inc=sem, by=16)
        if fence:
            P.pend(eng_name, sem, v)
        return v

    def col_reduce_dma(out_cell, op, wait_dve):
        d1 = dma("gpsimd", sc_dram.ap().rearrange("one (p c) -> (one p) c", c=4),
                 c4[:, :], waits=[("sDVE", wait_dve + 0),
                                  ("sAUX", P.val("sAUX")),
                                  ("sDVE", P.val("sDVE"))])
        d2 = dma("gpsimd", rowt[:, :], sc_dram[:, :], waits=[("sAUX", d1)])
        return P.add("vector", lambda e: e.tensor_reduce(
            out_cell, bass.AP(rowt, 0, [[512, 1], [4, 128]]), AXX, op),
            waits=[("sAUX", d2)], inc="sDVE")

    def bcast_dma(cell, slot, wait_dve):
        m = P.add("vector", lambda e: e.memset(rowt[:, :], 1.0),
                  waits=[("sDVE", wait_dve), ("sAUX", P.val("sAUX"))],
                  inc="sDVE")
        r = P.add("vector", lambda e: e.tensor_scalar(
            rowt[:, :], rowt[:, :], cell, None, ALU.mult), inc="sDVE")
        d1 = dma("gpsimd", sc_dram[:, :], rowt[:, :],
                 waits=[("sDVE", r), ("sAUX", P.val("sAUX"))])
        d2 = dma("gpsimd", sc[:, 4 * slot:4 * slot + 4],
                 sc_dram.ap().rearrange("one (p c) -> (one p) c", c=4),
                 waits=[("sAUX", d1)])
        return d2

    for t in range(NT):
        dma("sync", xs[:, 1024 * t:1024 * (t + 1)],
            x_my[128 * t:128 * (t + 1), :], fence=(t == NT - 1))
    dma("sync", fbrow, fb[:, :])
    dma("sync", idxl[:, :], idx_ln2[:, :])
    dma("sync", idxo[:, :], idx_ot[:, :])

    if STAGE < 0.15:
        for t in range(NT):
            dma("sync", out_my[128 * t:128 * (t + 1), :],
                xs[:, 1024 * t:1024 * (t + 1)],
                waits=[("sIN", P.val("sIN"))])
        P.mark("sync", "sSN", waits=[("sIN", P.val("sIN"))])
        return _finish(nc, P, es)

    P.add("gpsimd", lambda e: e.memset(identf[:, :], 1.0))
    P.add("gpsimd", lambda e: e.affine_select(identf[:, :], identf[:, :],
          [[-1, 128]], ALU.is_equal, 0.0, base=0, channel_multiplier=1))
    P.add("gpsimd", lambda e: e.memset(onef[:, :], 1.0))
    P.add("gpsimd", lambda e: e.memset(oneb[:, :], 1.0))
    P.add("gpsimd", lambda e: e.memset(onec[:, :], 1.0))
    P.add("gpsimd", lambda e: e.memset(msc[:, :], 0.0))
    P.add("gpsimd", lambda e: e.memset(vb[:, :], 1.0), inc="sPOOL")
    pool_const = P.val("sPOOL")
    P.add("vector", lambda e: e.tensor_copy(identb[:, :], identf[:, :]),
          waits=[("sPOOL", pool_const)], inc="sDVE")

    if STAGE < 0.19:
        for t in range(NT):
            dma("sync", out_my[128 * t:128 * (t + 1), :],
                xs[:, 1024 * t:1024 * (t + 1)],
                waits=[("sIN", P.val("sIN")), ("sDVE", P.val("sDVE"))])
        P.mark("sync", "sSN", waits=[("sIN", P.val("sIN"))])
        return _finish(nc, P, es)

    redA = []
    for o in range(8):
        st = fwst if o % 2 == 0 else abst
        w = [("sDVE", redA[o - 2])] if o >= 2 else ()
        d = dma("sync", st, fw[128 * o:128 * (o + 1), :], waits=w)
        r = P.add("vector", lambda e, s=st, o=o: e.reduce_sum(
            ws_a[:, o:o + 1], s, AXX, apply_absolute_value=True),
            waits=[("sIN", d)], inc="sDVE")
        redA.append(r)
    for j in range(4):
        P.add("vector", lambda e, j=j: e.reduce_sum(c4[:, j:j + 1],
              ws_a[:, 0:8], AXX), inc="sDVE")
    col_reduce_dma(scr[0:1, 0:1], ALU.add, P.val("sDVE"))
    P.add("vector", lambda e: e.tensor_scalar(scr[0:1, 0:1], scr[0:1, 0:1],
          1.0 / (1 << 20), None, ALU.mult), inc="sDVE")
    P.add("vector", lambda e: e.tensor_scalar(scr[0:1, 7:8], scr[0:1, 0:1],
          0.5, None, ALU.mult), inc="sDVE")
    hwb = bcast_dma(scr[0:1, 7:8], 0, P.val("sDVE"))
    hw_done = P.add("vector", lambda e: e.tensor_copy(mn_a[:, 1:2], sc[:, 0:1]),
                    waits=[("sAUX", hwb)], inc="sDVE")

    if STAGE < 0.199:
        for t in range(NT):
            dma("sync", out_my[128 * t:128 * (t + 1), :],
                xs[:, 1024 * t:1024 * (t + 1)],
                waits=[("sIN", P.val("sIN")), ("sDVE", P.val("sDVE"))])
        P.mark("sync", "sSN", waits=[("sIN", P.val("sIN"))])
        return _finish(nc, P, es)

    lastB = [0, 0]
    last_psT = 0
    last_tr = 0
    prev_qw = 0
    for o in range(8):
        st = fwst if o % 2 == 0 else abst
        d2 = dma("sync", st, fw[128 * o:128 * (o + 1), :],
                 waits=[("sDVE", max(redA[-1], lastB[o % 2]))])
        if True:
            sg_v = P.add("scalar", lambda e, s=st: e.activation(sgst, s, AF.Sign),
                         waits=[("sIN", d2), ("sDVE", max(hw_done, prev_qw))],
                         inc="sACT")
            ab_v = P.add("scalar", lambda e, s=st: e.activation(qf32[0], s, AF.Abs),
                         inc="sACT")
            g_v = P.add("vector", lambda e: e.tensor_scalar(
                qf32[0], qf32[0], sc[:, 0:1], None, ALU.is_gt),
                waits=[("sACT", ab_v)], inc="sDVE")
            qw_v = P.add("vector", lambda e: e.tensor_mul(qb_t[0], sgst, qf32[0]),
                         waits=[("sACT", sg_v), ("sDVE", g_v),
                                ("sPE", last_tr)], inc="sDVE")
        lastB[o % 2] = qw_v
        prev_qw = qw_v
        if STAGE < 0.21:
            continue
        for g in range(2):
            def trfn(e, g=g):
                for j in range(4):
                    e.transpose(psT[:, 128 * j:128 * (j + 1)],
                                qb_t[0][:, (4 * g + j) * 128:(4 * g + j + 1) * 128],
                                identb[:, :])
                return e.drain()
            tr_v = P.add("tensor", trfn,
                         waits=[("sDVE", max(qw_v, last_psT))], inc="sPE")
            last_tr = tr_v
            if True:
                last_psT = P.add("vector", lambda e, g=g, o=o: e.tensor_copy(
                    bass.AP(big, (4 * g) * 1024 + o * 128,
                            [[16384, 128], [1024, 4], [1, 128]]),
                    psT.ap()[:, 0:512].rearrange("p (j c) -> p j c", j=4)),
                    waits=[("sPE", tr_v)], inc="sDVE")

    def ln_stats(minmax):
        for t in range(NT):
            xt = xs[:, 1024 * t:1024 * (t + 1)]
            P.add("vector", lambda e, xt=xt, t=t: e.reduce_sum(
                m_a[:, t:t + 1], xt, AXX), inc="sDVE")
            P.add("vector", lambda e, xt=xt: e.tensor_mul(sqs, xt, xt),
                  inc="sDVE")
            P.add("vector", lambda e, t=t: e.reduce_sum(
                v_a[:, t:t + 1], sqs, AXX), inc="sDVE")
            if minmax:
                P.add("vector", lambda e, xt=xt, t=t: e.reduce_max(
                    mx_a[:, t:t + 1], xt, AXX), inc="sDVE")
                P.add("vector", lambda e, xt=xt, t=t: e.tensor_reduce(
                    mn_a[:, t:t + 1], xt, AXX, ALU.min), inc="sDVE")
        P.add("vector", lambda e: e.tensor_scalar(m_a[:, :], m_a[:, :],
              1.0 / D, None, ALU.mult), inc="sDVE")
        P.add("vector", lambda e: e.tensor_scalar(v_a[:, :], v_a[:, :],
              1.0 / D, None, ALU.mult), inc="sDVE")
        P.add("vector", lambda e: e.tensor_mul(ws_a[:, :], m_a[:, :], m_a[:, :]),
              inc="sDVE")
        P.add("vector", lambda e: e.tensor_sub(v_a[:, :], v_a[:, :], ws_a[:, :]),
              inc="sDVE")
        P.add("vector", lambda e: e.tensor_scalar(v_a[:, :], v_a[:, :],
              EPS, None, ALU.add), inc="sDVE")
        vd = P.val("sDVE")
        P.add("scalar", lambda e: e.activation(ws_a[:, :], v_a[:, :], AF.Ln),
              waits=[("sDVE", vd)], inc="sACT")
        P.add("scalar", lambda e: e.activation(rs_a[:, :], ws_a[:, :], AF.Exp,
              scale=-0.5), inc="sACT")
        return P.val("sACT")

    def iscale_collective():
        rsd = P.val("sACT")
        P.add("vector", lambda e: e.tensor_sub(mx_a[:, :], mx_a[:, :], m_a[:, :]),
              waits=[("sACT", rsd)], inc="sDVE")
        P.add("vector", lambda e: e.tensor_sub(mn_a[:, :], m_a[:, :], mn_a[:, :]),
              inc="sDVE")
        P.add("vector", lambda e: e.tensor_max(mx_a[:, :], mx_a[:, :], mn_a[:, :]),
              inc="sDVE")
        P.add("vector", lambda e: e.tensor_mul(mx_a[:, :], mx_a[:, :], rs_a[:, :]),
              inc="sDVE")
        for j in range(4):
            P.add("vector", lambda e, j=j: e.reduce_max(c4[:, j:j + 1],
                  mx_a[:, :], AXX), inc="sDVE")
        col_reduce_dma(scr[0:1, 1:2], ALU.max, P.val("sDVE"))
        lm = P.add("vector", lambda e: e.tensor_copy(msc[0:1, 0:1],
                   scr[0:1, 1:2]), inc="sDVE")
        d = dma("gpsimd", mx_in[:, :], msc[:, :], waits=[("sDVE", lm)])
        P.add("gpsimd", lambda e: e.collective_compute(
            "AllReduce", ALU.max, replica_groups=[list(range(8))],
            ins=[mx_in.ap().opt()], outs=[mx_out.ap().opt()]),
            waits=[("sAUX", d)], inc="sCC")
        gl = dma("gpsimd", msc[:, :], mx_out[:, :],
                 waits=[("sCC", P.val("sCC"))])
        P.add("vector", lambda e: e.tensor_scalar(scr[0:1, 2:3], msc[0:1, 0:1],
              1.0 / 127.0, None, ALU.mult), waits=[("sAUX", gl)], inc="sDVE")
        P.add("vector", lambda e: e.reciprocal(scr[0:1, 3:4], scr[0:1, 2:3]),
              inc="sDVE")
        P.add("vector", lambda e: e.tensor_mul(scr[0:1, 4:5], scr[0:1, 2:3],
              scr[0:1, 0:1]), inc="sDVE")
        P.add("vector", lambda e: e.tensor_scalar(scr[0:1, 5:6], scr[0:1, 4:5],
              0.5, None, ALU.mult), inc="sDVE")
        P.add("vector", lambda e: e.reciprocal(scr[0:1, 6:7], scr[0:1, 4:5]),
              inc="sDVE")
        P.add("vector", lambda e: e.tensor_scalar(biasq[:, :], fbrow,
              scr[0:1, 6:7], None, ALU.mult), inc="sDVE")
        sd = P.val("sDVE")
        bcast_dma(scr[0:1, 3:4], 1, sd)
        b2 = bcast_dma(scr[0:1, 5:6], 2, sd)
        P.mark("vector", "sDVE", waits=[("sAUX", b2)])
        P.add("vector", lambda e: e.tensor_scalar(rs_a[:, :], rs_a[:, :],
              sc[:, 4:5], None, ALU.mult), inc="sDVE")
        P.add("vector", lambda e: e.tensor_mul(nm_a[:, :], m_a[:, :],
              rs_a[:, :]), inc="sDVE")
        P.add("vector", lambda e: e.tensor_scalar(nm_a[:, :], nm_a[:, :],
              -1.0, None, ALU.mult), inc="sDVE")

    def ffn_pass():
        prev_stt = P.val("sDVE")
        rv_hist = [0, 0]
        psT_ev = P.val("sDVE")
        for t in range(NT):
            par = t % 2
            xt = xs[:, 1024 * t:1024 * (t + 1)]
            if True:
                qv = P.add("scalar", lambda e, xt=xt, t=t, par=par: e.activation(
                    qf32[par], xt, AF.Identity, bias=nm_a[:, t:t + 1],
                    scale=rs_a[:, t:t + 1]), waits=[("sDVE", rv_hist[par])],
                    inc="sACT")
                rv = P.add("vector", lambda e, par=par: e.tensor_scalar(
                    qb_t[par], qf32[par], MAGIC, MAGIC, ALU.add, ALU.subtract),
                    waits=[("sACT", qv)], inc="sDVE")
            rv_hist[par] = rv
            evs = []
            for g in range(2):
                tv = P.add("tensor", lambda e, g=g, par=par: [
                    e.transpose(psT[:, 128 * j:128 * (j + 1)],
                                qb_t[par][:, (4 * g + j) * 128:(4 * g + j + 1) * 128],
                                identb[:, :]) for j in range(4)][-1],
                    waits=[("sDVE", max(rv, psT_ev))], inc="sPE")
                psT_ev = P.add("vector", lambda e, g=g, par=par: e.tensor_copy(
                    qT_t[par][:, 512 * g:512 * (g + 1)], psT[:, 0:512]),
                    waits=[("sPE", tv)], inc="sDVE")
                evs.append(psT_ev)
            stts = [prev_stt]
            for h2 in range(2):
                ps = psC if h2 == 0 else psD
                mm = P.add("tensor", lambda e, par=par, h2=h2, ps=ps: [
                    e.matmul(ps[:, :],
                             qT_t[par][:, 128 * k:128 * (k + 1)],
                             big[:, 1024 * k + 512 * h2:
                                 1024 * k + 512 * (h2 + 1)],
                             start=(k == 0), stop=False) for k in range(8)][-1],
                    waits=[("sDVE", max(evs + [prev_stt]))], inc="sPE")
                mm = P.add("tensor", lambda e, h2=h2, ps=ps: e.matmul(
                    ps[:, :], oneb[:, :], biasq[:, 512 * h2:512 * (h2 + 1)],
                    start=False, stop=True), inc="sPE")
                stts.append(P.add("vector", lambda e, xt=xt, h2=h2, ps=ps:
                    e.scalar_tensor_tensor(
                        xt[:, 512 * h2:512 * (h2 + 1)], ps[:, :], sc[:, 8:9],
                        xt[:, 512 * h2:512 * (h2 + 1)], ALU.mult, ALU.add),
                    waits=[("sPE", mm)], inc="sDVE"))
            prev_stt = max(stts)

    P.barrier()
    if STAGE >= 0.3:
        ln_stats(minmax=True)
        iscale_collective()
        P.barrier()
    if STAGE >= 1:
        ffn_pass()
        P.barrier()

    if STAGE >= 2:
        for k in range(8):
            dma("sync", qwT_dram[:, 1024 * k:1024 * (k + 1)],
                big[:, 1024 * k:1024 * (k + 1)], fence=(k == 7))
        qwT_saved = P.val("sIN")
        ln_stats(minmax=False)
        rsd = P.val("sACT")
        P.add("vector", lambda e: e.tensor_mul(nm_a[:, :], m_a[:, :],
              rs_a[:, :]), waits=[("sACT", rsd)], inc="sDVE")
        st_done = P.add("vector", lambda e: e.tensor_scalar(nm_a[:, :],
                        nm_a[:, :], -1.0, None, ALU.mult), inc="sDVE")
        ln2dma = []
        for t in range(NT):
            par = t % 2
            xt = xs[:, 1024 * t:1024 * (t + 1)]
            w = [("sDVE", st_done)]
            if t >= 2:
                w.append(("sIN", ln2dma[t - 2]))
            av = P.add("scalar", lambda e, xt=xt, t=t, par=par: e.activation(
                ln2st[:, 1024 * par:1024 * (par + 1)], xt, AF.Identity,
                bias=nm_a[:, t:t + 1], scale=rs_a[:, t:t + 1]),
                waits=w, inc="sACT")
            d = dma("sync", ln2_in[128 * t:128 * (t + 1), :],
                    ln2st[:, 1024 * par:1024 * (par + 1)],
                    waits=[("sACT", av)], fence=(t >= NT - 2))
            ln2dma.append(d)
        P.add("gpsimd", lambda e: e.collective_compute(
            "AllGather", ALU.bypass, replica_groups=[list(range(8))],
            ins=[ln2_in.ap().opt()], outs=[ln2_out.ap().opt()]),
            waits=[("sIN", ln2dma[-1]), ("sAUX", P.val("sAUX"))], inc="sCC")
        ag1 = P.val("sCC")
        wic = 0
        for f in range(8):
            d = dma("sync", wist, wi_my[128 * f:128 * (f + 1), :],
                    waits=[("sIN", qwT_saved), ("sDVE", wic)])
            wic = P.add("vector", lambda e, f=f: e.tensor_copy(
                big[:, 1536 * f:1536 * (f + 1)], wist),
                waits=[("sIN", d)], inc="sDVE")
        bd_ = dma("sync", birow, bi_my[:, :], waits=[("sDVE", wic)])
        bcv = P.add("vector", lambda e: e.tensor_copy(birb, birow),
                    waits=[("sIN", bd_)], inc="sDVE")
        btr = P.add("tensor", lambda e: [
            e.transpose(psT[0:128, 2 * j:2 * j + 1],
                        birb[:, 128 * j:128 * (j + 1)],
                        identb[0:1, 0:1]) for j in range(8)][-1],
            waits=[("sDVE", bcv)], inc="sPE")
        P.add("vector", lambda e: e.tensor_copy(
            bcol[:, :], psT.ap()[:, 0:16].rearrange("p (j two) -> p j two",
                                                    two=2)[:, :, 0]),
            waits=[("sPE", btr)], inc="sDVE")
        P.add("vector", lambda e: e.tensor_scalar(bcol[:, 0:4], bcol[:, 0:4],
              0.125, None, ALU.mult), inc="sDVE")
        P.barrier()

        glob_tr = []
        for bd in range(NB):
            for tt in range(4):
                par = tt % 2
                g = 4 * bd + tt
                w = [("sCC", ag1)]
                if g >= 2:
                    w.append(("sPE", glob_tr[g - 2]))
                d = P.add("gpsimd", lambda e, g=g, par=par: e.indirect_dma_start(
                    ln2st[:, 1024 * par:1024 * (par + 1)], None,
                    ln2_out.ap(), bass.IndirectOffsetOnAxis(idxl[:, g:g + 1], 0)),
                    waits=w, inc="sAUX", by=16)
                for gg in range(2):
                    tv = P.add("tensor", lambda e, gg=gg, par=par: [
                        e.transpose(psT[:, 128 * j:128 * (j + 1)],
                                    ln2st[:, 1024 * par + (4 * gg + j) * 128:
                                          1024 * par + (4 * gg + j + 1) * 128],
                                    identb[:, :]) for j in range(4)][-1],
                        waits=[("sAUX", d), ("sDVE", P.val("sDVE"))], inc="sPE")
                    P.add("vector", lambda e, gg=gg, tt=tt: e.tensor_copy(
                        bass.AP(big, 12288 + (4 * gg) * 512 + 128 * tt,
                                [[16384, 128], [512, 4], [1, 128]]),
                        psT.ap()[:, 0:512].rearrange("p (j c) -> p j c", j=4)),
                        waits=[("sPE", tv)], inc="sDVE")
                glob_tr.append(P.val("sPE"))
            lnt_done = P.val("sDVE")
            for oc in range(8):
                ps = psC if oc % 2 == 0 else psD
                mm = P.add("tensor", lambda e, oc=oc, ps=ps: [
                    e.matmul(ps[:, :],
                             big[:, 1536 * f + 128 * oc:1536 * f + 128 * (oc + 1)],
                             big[:, 12288 + 512 * f:12288 + 512 * (f + 1)],
                             start=(f == 0), stop=(f == 7)) for f in range(8)][-1],
                    waits=[("sDVE", lnt_done), ("sACT", P.val("sACT"))],
                    inc="sPE")
                isq = oc < 4
                dest = (qt if isq else kt)
                doff = (oc if isq else oc - 4) * 4096 + 512 * bd
                P.add("scalar", lambda e, ps=ps, dest=dest, doff=doff, oc=oc,
                      isq=isq: e.activation(
                          dest[:, doff:doff + 512], ps[:, :], AF.Identity,
                          bias=bcol[:, oc:oc + 1], scale=(0.125 if isq else 1.0)),
                      waits=[("sPE", mm)], inc="sACT")
            for tt in range(4):
                ps = psE if tt % 2 == 0 else psF
                kc = 4 * bd + tt
                mm = P.add("tensor", lambda e, tt=tt, ps=ps: [
                    e.matmul(ps[:, :],
                             big[:, 12288 + 512 * f + 128 * tt:
                                  12288 + 512 * f + 128 * (tt + 1)],
                             big[:, 1536 * f + 1024:1536 * (f + 1)],
                             start=(f == 0), stop=False) for f in range(8)][-1],
                    waits=[("sDVE", P.val("sDVE"))], inc="sPE")
                mm = P.add("tensor", lambda e, ps=ps: e.matmul(
                    ps[:, :], oneb[:, :], bvrow, start=False, stop=True),
                    inc="sPE")
                P.add("vector", lambda e, ps=ps, kc=kc: e.tensor_copy(
                    bass.AP(vb, 65 * kc, [[8 * 32 * 65, 128], [2080, 8], [1, 64]]),
                    ps.ap().rearrange("p (j c) -> p j c", j=8)),
                    waits=[("sPE", mm)], inc="sDVE")
        P.barrier()

        for pp in range(4):
            for bd in range(NB):
                kmax = 4 * bd + 4
                qbase = 4096 * pp + 512 * bd
                exp_done = [0] * (kmax + 1)
                pool_done = [0] * (kmax + 1)
                av_done = [0, 0]
                for kc in range(kmax):
                    j0 = max(0, kc - 4 * bd)
                    qo = 128 * j0
                    par = kc % 2
                    psCk = psC if par == 0 else psA.ap()[:, 0:512]
                    psDk = psD if par == 0 else psA.ap()[:, 512:1024]
                    eA = expb[:, 0:512]
                    eB = expb[:, 512:1024]
                    w = [("sACT", exp_done[kc - 2] if kc >= 2 else 0),
                         ("sPE", av_done[0])]
                    sA = P.add("tensor", lambda e, pp=pp, kc=kc, qo=qo,
                               qbase=qbase, psCk=psCk: e.matmul(
                                   psCk[:, qo:512],
                                   kt[0:64, 4096 * pp + 128 * kc:
                                      4096 * pp + 128 * (kc + 1)],
                                   qt[0:64, qbase + qo:qbase + 512],
                                   start=True, stop=True), waits=w, inc="sPE")
                    sB = P.add("tensor", lambda e, pp=pp, kc=kc, qo=qo,
                               qbase=qbase, psDk=psDk: e.matmul(
                                   psDk[:, qo:512],
                                   kt[64:128, 4096 * pp + 128 * kc:
                                      4096 * pp + 128 * (kc + 1)],
                                   qt[64:128, qbase + qo:qbase + 512],
                                   start=True, stop=True,
                                   tile_position=(64, 0)), inc="sPE")
                    we = [("sPE", sA)]
                    if kc >= 2:
                        we.append(("sPOOL", pool_done[kc - 2]))
                    P.add("scalar", lambda e, qo=qo, eA=eA, psCk=psCk:
                          e.activation(eA[:, qo:512], psCk[:, qo:512], AF.Exp),
                          waits=we, inc="sACT")
                    ed = P.add("scalar", lambda e, qo=qo, eB=eB, psDk=psDk:
                               e.activation(eB[:, qo:512], psDk[:, qo:512],
                                            AF.Exp),
                               waits=[("sPE", sB)], inc="sACT")
                    exp_done[kc] = ed
                    wv = [("sACT", ed)]
                    if kc >= 4 * bd:
                        P.add("gpsimd", lambda e, eA=eA, qo=qo: e.affine_select(
                            eA[:, qo:qo + 128], eA[:, qo:qo + 128], [[1, 128]],
                            ALU.is_ge, 0.0, base=0, channel_multiplier=-1),
                            waits=[("sACT", ed)], inc="sPOOL")
                        pd = P.add("gpsimd", lambda e, eB=eB, qo=qo:
                                   e.affine_select(
                                       eB[:, qo:qo + 128], eB[:, qo:qo + 128],
                                       [[1, 128]], ALU.is_ge, 0.0, base=0,
                                       channel_multiplier=-1), inc="sPOOL")
                        pool_done[kc] = pd
                        wv.append(("sPOOL", pd))
                    else:
                        pool_done[kc] = pool_done[kc - 1] if kc else 0
                    P.add("tensor", lambda e, pp=pp, kc=kc, qo=qo, eA=eA,
                          kmax=kmax: e.matmul(
                              psE[0:65, qo:512],
                              vb[:, 2080 * (2 * pp) + 65 * kc:
                                 2080 * (2 * pp) + 65 * kc + 65],
                              eA[:, qo:512],
                              start=(kc == 0), stop=(kc == kmax - 1)),
                          waits=wv, inc="sPE")
                    av_done[par] = P.add("tensor", lambda e, pp=pp, kc=kc,
                        qo=qo, eB=eB, kmax=kmax: e.matmul(
                            psF[0:65, qo:512],
                            vb[:, 2080 * (2 * pp + 1) + 65 * kc:
                               2080 * (2 * pp + 1) + 65 * kc + 65],
                            eB[:, qo:512],
                            start=(kc == 0), stop=(kc == kmax - 1)),
                        inc="sPE")
                last = P.val("sPE")

                def lp(fn):
                    def g(e):
                        with nc.allow_low_precision("softmax denom in bf16"):
                            return fn(e)
                    return g
                P.add("vector", lp(lambda e: e.reciprocal(rD[0:1, 0:512],
                      psE[64:65, :])), waits=[("sPE", last)], inc="sDVE")
                r2 = P.add("vector", lp(lambda e: e.reciprocal(
                    rD[0:1, 512:1024], psF[64:65, :])), inc="sDVE")
                P.add("tensor", lambda e: e.matmul(
                    psC[0:64, :], oneb[0:1, 0:64], rD[0:1, 0:512],
                    start=True, stop=True),
                    waits=[("sDVE", r2), ("sACT", P.val("sACT"))], inc="sPE")
                b2 = P.add("tensor", lambda e: e.matmul(
                    psC[64:128, :], oneb[0:1, 0:64], rD[0:1, 512:1024],
                    start=True, stop=True, tile_position=(0, 64)), inc="sPE")
                rbv = P.add("vector", lambda e: e.tensor_copy(rb[:, :],
                            psC[:, :]), waits=[("sPE", b2)], inc="sDVE")
                P.add("vector", lambda e, qbase=qbase: e.tensor_mul(
                    big[0:64, qbase:qbase + 512], psE[0:64, :], rb[0:64, :]),
                    waits=[("sDVE", rbv)], inc="sDVE")
                n2 = P.add("vector", lambda e, qbase=qbase: e.tensor_mul(
                    big[64:128, qbase:qbase + 512], psF[0:64, :],
                    rb[64:128, :]), inc="sDVE")
                P.pend("tensor", "sDVE", n2)
        P.barrier()

        for p4 in range(4):
            for hh in range(2):
                dma("sync", ot_in[128 * (4 * hh + p4):128 * (4 * hh + p4 + 1), :],
                    big[:, 4096 * p4 + 2048 * hh:4096 * p4 + 2048 * (hh + 1)],
                    fence=(p4 == 3 and hh == 1))
        otd = P.val("sIN")
        P.add("gpsimd", lambda e: e.collective_compute(
            "AllGather", ALU.bypass, replica_groups=[list(range(8))],
            ins=[ot_in.ap().opt()], outs=[ot_out.ap().opt()]),
            waits=[("sIN", otd), ("sAUX", P.val("sAUX"))], inc="sCC")
        ag2 = P.val("sCC")
        got = 0
        for g in range(8):
            dest = kt if g < 4 else qt
            doff = 2048 * (g % 4)
            got = P.add("gpsimd", lambda e, g=g, dest=dest, doff=doff:
                        e.indirect_dma_start(
                            dest[:, doff:doff + 2048], None, ot_out.ap(),
                            bass.IndirectOffsetOnAxis(idxo[:, g:g + 1], 0)),
                        waits=[("sCC", ag2)], inc="sAUX", by=16)
            if g == 7:
                P.pend("gpsimd", "sAUX", got)
        woc = 0
        for j in range(8):
            d = dma("sync", worow, wo_cat[128 * j:128 * (j + 1), :],
                    waits=[("sIN", otd), ("sDVE", woc)])
            woc = P.add("vector", lambda e, j=j: e.tensor_copy(
                qt[:, 8192 + 1024 * j:8192 + 1024 * (j + 1)], worow),
                waits=[("sIN", d)], inc="sDVE")
        bod = dma("sync", worow[0:1, :], bo[:, :], waits=[("sDVE", woc)])
        bocast = P.add("vector", lambda e: e.tensor_copy(borow[:, :],
                       worow[0:1, :]), waits=[("sIN", bod)], inc="sDVE")
        P.barrier()

        prev = max(P.val("sDVE"), bocast)
        for t in range(NT):
            mm = P.add("tensor", lambda e, t=t: [
                e.matmul(psA[:, 512 * h2:512 * (h2 + 1)],
                         (kt if j < 4 else qt)[:, 2048 * (j % 4) + 128 * t:
                                               2048 * (j % 4) + 128 * (t + 1)],
                         qt[:, 8192 + 1024 * j + 512 * h2:
                            8192 + 1024 * j + 512 * (h2 + 1)],
                         start=(j == 0), stop=False)
                for j in range(8) for h2 in range(2)][-1],
                waits=[("sAUX", got), ("sDVE", prev)], inc="sPE")
            mm = P.add("tensor", lambda e: [
                e.matmul(psA[:, 512 * h2:512 * (h2 + 1)], oneb[:, :],
                         borow[:, 512 * h2:512 * (h2 + 1)],
                         start=False, stop=True) for h2 in range(2)][-1],
                inc="sPE")
            prev = P.add("vector", lambda e, t=t: [
                e.tensor_add(
                    xs[:, 1024 * t + 512 * h2:1024 * t + 512 * (h2 + 1)],
                    xs[:, 1024 * t + 512 * h2:1024 * t + 512 * (h2 + 1)],
                    psA[:, 512 * h2:512 * (h2 + 1)]) for h2 in range(2)][-1],
                waits=[("sPE", mm)], inc="sDVE")
        P.barrier()

    if STAGE >= 3:
        for k in range(8):
            dma("sync", big[:, 1024 * k:1024 * (k + 1)],
                qwT_dram[:, 1024 * k:1024 * (k + 1)], fence=(k == 7))
        qw2 = dma("sync", fbrow, fb[:, :])
        ln_stats(minmax=True)
        iscale_collective()
        P.barrier()
        P.pend("tensor", "sIN", qw2)
        ffn_pass()
        P.barrier()

    for t in range(NT):
        dma("sync", out_my[128 * t:128 * (t + 1), :],
            xs[:, 1024 * t:1024 * (t + 1)], fence=False)
    P.mark("sync", "sSN", waits=[("sIN", P.val("sIN"))])

    return _finish(nc, P, es)


def _finish(nc, P, es):
    with (
        nc.semaphore("sIN") as sIN, nc.semaphore("sAUX") as sAUX,
        nc.semaphore("sCC") as sCC, nc.semaphore("sPE") as sPE,
        nc.semaphore("sDVE") as sDVE, nc.semaphore("sACT") as sACT,
        nc.semaphore("sPOOL") as sPOOL, nc.semaphore("sSN") as sSN,
        nc.semaphore("sGN") as sGN,
        nc.Block() as block,
    ):
        sems = dict(sIN=sIN, sAUX=sAUX, sCC=sCC, sPE=sPE, sDVE=sDVE,
                    sACT=sACT, sPOOL=sPOOL, sSN=sSN, sGN=sGN)
        P.emit(block, sems)
    es.close()
    return nc


_NC_CACHE = {}


def kernel(L0_attn_x, L0_attn_w_in, L0_attn_b_in, L0_attn_w_out, L0_attn_b_out,
           L0_ffn_weight, L0_ffn_bias):
    x = np.ascontiguousarray(np.asarray(L0_attn_x, dtype=np.float32))
    w_in = np.ascontiguousarray(np.asarray(L0_attn_w_in, dtype=np.float32))
    b_in = np.asarray(L0_attn_b_in, dtype=np.float32).reshape(1, 3072)
    w_out = np.ascontiguousarray(np.asarray(L0_attn_w_out, dtype=np.float32))
    b_out = np.asarray(L0_attn_b_out, dtype=np.float32).reshape(1, D)
    fw = np.ascontiguousarray(np.asarray(L0_ffn_weight, dtype=np.float32))
    fb = np.asarray(L0_ffn_bias, dtype=np.float32).reshape(1, D)

    in_maps = []
    for c in range(8):
        b, h = c // 2, c % 2
        qs = slice(512 * h, 512 * h + 512)
        ks = slice(1024 + 512 * h, 1024 + 512 * h + 512)
        vs = slice(2048 + 512 * h, 2048 + 512 * h + 512)
        wi = np.ascontiguousarray(
            np.concatenate([w_in[:, qs], w_in[:, ks], w_in[:, vs]], axis=1))
        bi = np.ascontiguousarray(
            np.concatenate([b_in[:, qs], b_in[:, ks], b_in[:, vs]], axis=1))
        wo = np.ascontiguousarray(np.concatenate(
            [w_out[512 * h:512 * h + 512, :],
             w_out[512 * (1 - h):512 * (1 - h) + 512, :]], axis=0))
        idxl_np = (b * 4096 + np.arange(4096, dtype=np.uint32)
                   ).reshape(32, 128).T.copy()
        rows = np.concatenate([
            c * 1024 + 512 * h + np.arange(512, dtype=np.uint32),
            (c ^ 1) * 1024 + 512 * h + np.arange(512, dtype=np.uint32)])
        idxo_np = rows.reshape(8, 128).T.copy()
        in_maps.append(dict(
            x_my=np.ascontiguousarray(x[b, 2048 * h:2048 * h + 2048, :]),
            wi_my=wi, bi_my=bi, wo_cat=wo, bo=b_out,
            fw=fw, fb=fb, idx_ln2=idxl_np, idx_ot=idxo_np))

    if "nc" not in _NC_CACHE:
        _NC_CACHE["nc"] = build_nc()
    res = run_bass_kernel_spmd(_NC_CACHE["nc"], in_maps, core_ids=list(range(8)))
    out = np.empty((B, S, D), dtype=np.float32)
    for c in range(8):
        b, h = c // 2, c % 2
        out[b, 2048 * h:2048 * h + 2048, :] = res.results[c]["out_my"]
    return out


# version 06
# speedup vs baseline: 1.0620x; 1.0470x over previous
import numpy as np
import ml_dtypes
import concourse.bass as bass
import concourse.mybir as mybir
from concourse.bass_utils import run_bass_kernel_spmd

F32 = mybir.dt.float32
BF16 = mybir.dt.bfloat16
U32 = mybir.dt.uint32
AF = mybir.ActivationFunctionType
ALU = mybir.AluOpType
AXX = mybir.AxisListType.X

B, S, D = 4, 4096, 1024
TOK = 2048
NT = TOK // 128
NB = S // 512
MAGIC = 1.5 * 2.0 ** 23
EPS = 1e-5
STAGE = 3
RACE_DETECT = False

ENGINES = ("sync", "gpsimd", "tensor", "vector", "scalar")
ESEM = {"sync": "sSN", "gpsimd": "sGN", "tensor": "sPE",
        "vector": "sDVE", "scalar": "sACT"}


class Prog:

    def __init__(self):
        self.ops = {e: [] for e in ENGINES}
        self.c = {}
        self.pending = {e: {} for e in ENGINES}
        self.last = {}

    def val(self, sem):
        return self.c.get(sem, 0)

    SERIAL = ("vector", "scalar")

    def add(self, eng, fn, waits=(), inc=None, by=1, selfinc=False):
        w = dict(self.pending[eng])
        self.pending[eng] = {}
        for s, v in waits:
            w[s] = max(w.get(s, 0), v)
        if eng in self.SERIAL:
            prev = self.last.get(eng)
            if prev is not None:
                w[prev[0]] = max(w.get(prev[0], 0), prev[1])
        if inc is not None:
            self.c[inc] = self.c.get(inc, 0) + by
            if eng in self.SERIAL:
                self.last[eng] = (inc, self.c[inc])
        self.ops[eng].append((fn, w, None if selfinc else inc, by))
        return self.c.get(inc) if inc is not None else None

    tensor_mark_fn = None

    def mark(self, eng, sem, waits=()):
        if eng == "tensor":
            return self.add(eng, self.tensor_mark_fn, waits=waits, inc=sem)
        return self.add(eng, lambda en, s=sem: en.sem_inc(self._sems[s], 1),
                        waits=waits, inc=sem, selfinc=True)

    def pend(self, eng, sem, v):
        self.pending[eng][sem] = max(self.pending[eng].get(sem, 0), v)

    def barrier(self):
        marks = []
        for e in ENGINES:
            w = []
            if e == "sync":
                w = [("sIN", self.val("sIN"))]
            elif e == "gpsimd":
                w = [("sAUX", self.val("sAUX")), ("sCC", self.val("sCC"))]
            elif e == "tensor":
                w = [("sDVE", self.val("sDVE")), ("sACT", self.val("sACT"))]
            v = self.mark(e, ESEM[e], waits=w)
            marks.append((ESEM[e], v))
        for e in ENGINES:
            for s, v in marks:
                self.pend(e, s, v)

    def emit(self, block, sems):
        self._sems = sems
        def mk(engname):
            def run(eng):
                seen = {}
                for fn, waits, inc, by in self.ops[engname]:
                    for s, v in waits.items():
                        if v > 0 and seen.get(s, -1) < v:
                            eng.wait_ge(sems[s], v)
                            seen[s] = v
                    ins = fn(eng)
                    if inc is not None:
                        ins.then_inc(sems[inc], by)
            return run
        block.sync(mk("sync"))
        block.gpsimd(mk("gpsimd"))
        block.tensor(mk("tensor"))
        block.vector(mk("vector"))
        block.scalar(mk("scalar"))


def build_nc():
    nc = bass.Bass("TRN2", target_bir_lowering=False,
                   detect_race_conditions=RACE_DETECT)

    x_my = nc.dram_tensor("x_my", [TOK, D], F32, kind="ExternalInput")
    wi_my = nc.dram_tensor("wi_my", [D, 1536], F32, kind="ExternalInput")
    bi_my = nc.dram_tensor("bi_my", [1, 1536], F32, kind="ExternalInput")
    wo_cat = nc.dram_tensor("wo_cat", [D, D], F32, kind="ExternalInput")
    bo = nc.dram_tensor("bo", [1, D], F32, kind="ExternalInput")
    fw = nc.dram_tensor("fw", [D, D], F32, kind="ExternalInput")
    fb = nc.dram_tensor("fb", [1, D], F32, kind="ExternalInput")
    idx_ln2 = nc.dram_tensor("idx_ln2", [128, 32], U32, kind="ExternalInput")
    idx_ot = nc.dram_tensor("idx_ot", [128, 8], U32, kind="ExternalInput")
    out_my = nc.dram_tensor("out_my", [TOK, D], F32, kind="ExternalOutput")

    qwT_dram = nc.dram_tensor("qwT_dram", [128, 8192], BF16, kind="Internal")
    ln2_in = nc.dram_tensor("ln2_in", [TOK, D], BF16, kind="Internal")
    ln2_out = nc.dram_tensor("ln2_out", [8 * TOK, D], BF16, kind="Internal")
    ot_in = nc.dram_tensor("ot_in", [1024, TOK], BF16, kind="Internal")
    ot_out = nc.dram_tensor("ot_out", [8192, TOK], BF16, kind="Internal")
    sc_dram = nc.dram_tensor("sc_dram", [1, 512], F32, kind="Internal")
    mx_in = nc.dram_tensor("mx_in", [1, 16], F32, kind="Internal")
    mx_out = nc.dram_tensor("mx_out", [1, 16], F32, kind="Internal")

    import contextlib
    es = contextlib.ExitStack()
    SB = lambda nm, sh, dt: es.enter_context(nc.sbuf_tensor(nm, sh, dt))
    PS = lambda nm, sh, dt: es.enter_context(nc.psum_tensor(nm, sh, dt))

    xs = SB("xs", [128, NT * 1024], F32)
    qt = SB("qt", [128, 16384], BF16)
    kt = SB("kt", [128, 16384], BF16)
    vb = SB("vb", [128, 8 * 32 * 65], BF16)
    big = SB("big", [128, 16384], BF16)
    identf = SB("identf", [128, 128], F32)
    identb = SB("identb", [128, 128], BF16)
    ln2st = SB("ln2st", [128, 2 * 1024], BF16)
    expb = SB("expb", [128, 2 * 512], BF16)
    rb = SB("rb", [128, 512], BF16)
    m_a = SB("m_a", [128, 16], F32)
    v_a = SB("v_a", [128, 16], F32)
    rs_a = SB("rs_a", [128, 16], F32)
    nm_a = SB("nm_a", [128, 16], F32)
    mx_a = SB("mx_a", [128, 16], F32)
    mn_a = SB("mn_a", [128, 16], F32)
    ws_a = SB("ws_a", [128, 16], F32)
    sc = SB("sc", [128, 12], F32)
    c4 = SB("c4", [128, 4], F32)
    bcol = SB("bcol", [128, 8], F32)
    scr = SB("scr", [1, 16], F32)
    msc = SB("msc", [1, 16], F32)
    onef = SB("onef", [1, 128], F32)
    oneb = SB("oneb", [1, 128], BF16)
    onec = SB("onec", [128, 1], F32)
    rowt = SB("rowt", [1, 512], F32)
    rowb = SB("rowb", [1, 1536], BF16)
    biasq = rowb.ap()[0:1, 0:1024]
    birb = rowb.ap()
    bvrow = rowb.ap()[0:1, 1024:1536]
    rD = rowb.ap()[0:1, 0:1024]
    borow = rowb.ap()[0:1, 0:1024]
    idxl = SB("idxl", [128, 32], U32)
    idxo = SB("idxo", [128, 8], U32)

    psA = PS("psA", [128, 1024], F32)
    psCD = PS("psCD", [128, 1024], F32)
    psC = psCD.ap()[:, 0:512]
    psD = psCD.ap()[:, 512:1024]
    psE = PS("psE", [128, 512], F32)
    psF = PS("psF", [128, 512], F32)
    psT = PS("psT", [128, 1024], BF16)

    qtf = qt.ap().bitcast(F32)
    qf32 = [qtf[:, 0:1024], qtf[:, 1024:2048]]
    sqs = qtf[:, 1024:2048]
    fwst = qtf[:, 2048:3072]
    abst = qtf[:, 3072:4096]
    sgst = qtf[:, 4096:5120]
    wist = qtf[:, 5120:6656]
    birow = qtf[0:1, 5120:6656]
    fbrow = qtf[0:1, 6656:7680]
    worow = big.ap().bitcast(F32)[:, 0:1024]
    ktb = kt.ap()
    qb_t = [ktb[:, 0:1024], ktb[:, 1024:2048]]
    qT_t = [ktb[:, 2048:3072], ktb[:, 3072:4096]]

    P = Prog()
    P.tensor_mark_fn = lambda e: e.transpose(psT[:, 0:128], identb[:, :],
                                             identb[:, :])

    def dma(eng_name, dst, src, waits=(), sem=None, slow=False, fence=True):
        sem = sem or ("sIN" if eng_name == "sync" else "sAUX")
        v = P.add(eng_name, lambda e, d=dst, s=src, sl=slow: e.dma_start(
            out=d, in_=s, allow_slow_non_contiguous=sl), waits=waits,
            inc=sem, by=16)
        if fence:
            P.pend(eng_name, sem, v)
        return v

    def col_reduce_dma(out_cell, op, wait_dve):
        d1 = dma("gpsimd", sc_dram.ap().rearrange("one (p c) -> (one p) c", c=4),
                 c4[:, :], waits=[("sDVE", wait_dve + 0),
                                  ("sAUX", P.val("sAUX")),
                                  ("sDVE", P.val("sDVE"))])
        d2 = dma("gpsimd", rowt[:, :], sc_dram[:, :], waits=[("sAUX", d1)])
        return P.add("vector", lambda e: e.tensor_reduce(
            out_cell, bass.AP(rowt, 0, [[512, 1], [4, 128]]), AXX, op),
            waits=[("sAUX", d2)], inc="sDVE")

    def bcast_dma(cell, slot, wait_dve):
        m = P.add("vector", lambda e: e.memset(rowt[:, :], 1.0),
                  waits=[("sDVE", wait_dve), ("sAUX", P.val("sAUX"))],
                  inc="sDVE")
        r = P.add("vector", lambda e: e.tensor_scalar(
            rowt[:, :], rowt[:, :], cell, None, ALU.mult), inc="sDVE")
        d1 = dma("gpsimd", sc_dram[:, :], rowt[:, :],
                 waits=[("sDVE", r), ("sAUX", P.val("sAUX"))])
        d2 = dma("gpsimd", sc[:, 4 * slot:4 * slot + 4],
                 sc_dram.ap().rearrange("one (p c) -> (one p) c", c=4),
                 waits=[("sAUX", d1)])
        return d2

    for t in range(NT):
        dma("sync", xs[:, 1024 * t:1024 * (t + 1)],
            x_my[128 * t:128 * (t + 1), :], fence=(t == NT - 1))
    dma("sync", fbrow, fb[:, :])
    dma("sync", idxl[:, :], idx_ln2[:, :])
    dma("sync", idxo[:, :], idx_ot[:, :])

    if STAGE < 0.15:
        for t in range(NT):
            dma("sync", out_my[128 * t:128 * (t + 1), :],
                xs[:, 1024 * t:1024 * (t + 1)],
                waits=[("sIN", P.val("sIN"))])
        P.mark("sync", "sSN", waits=[("sIN", P.val("sIN"))])
        return _finish(nc, P, es)

    P.add("gpsimd", lambda e: e.memset(identf[:, :], 1.0))
    P.add("gpsimd", lambda e: e.affine_select(identf[:, :], identf[:, :],
          [[-1, 128]], ALU.is_equal, 0.0, base=0, channel_multiplier=1))
    P.add("gpsimd", lambda e: e.memset(onef[:, :], 1.0))
    P.add("gpsimd", lambda e: e.memset(oneb[:, :], 1.0))
    P.add("gpsimd", lambda e: e.memset(onec[:, :], 1.0))
    P.add("gpsimd", lambda e: e.memset(msc[:, :], 0.0))
    P.add("gpsimd", lambda e: e.memset(vb[:, :], 1.0), inc="sPOOL")
    pool_const = P.val("sPOOL")
    P.add("vector", lambda e: e.tensor_copy(identb[:, :], identf[:, :]),
          waits=[("sPOOL", pool_const)], inc="sDVE")

    if STAGE < 0.19:
        for t in range(NT):
            dma("sync", out_my[128 * t:128 * (t + 1), :],
                xs[:, 1024 * t:1024 * (t + 1)],
                waits=[("sIN", P.val("sIN")), ("sDVE", P.val("sDVE"))])
        P.mark("sync", "sSN", waits=[("sIN", P.val("sIN"))])
        return _finish(nc, P, es)

    redA = []
    for o in range(8):
        st = fwst if o % 2 == 0 else abst
        w = [("sDVE", redA[o - 2])] if o >= 2 else ()
        d = dma("sync", st, fw[128 * o:128 * (o + 1), :], waits=w)
        r = P.add("vector", lambda e, s=st, o=o: e.reduce_sum(
            ws_a[:, o:o + 1], s, AXX, apply_absolute_value=True),
            waits=[("sIN", d)], inc="sDVE")
        redA.append(r)
    for j in range(4):
        P.add("vector", lambda e, j=j: e.reduce_sum(c4[:, j:j + 1],
              ws_a[:, 0:8], AXX), inc="sDVE")
    col_reduce_dma(scr[0:1, 0:1], ALU.add, P.val("sDVE"))
    P.add("vector", lambda e: e.tensor_scalar(scr[0:1, 0:1], scr[0:1, 0:1],
          1.0 / (1 << 20), None, ALU.mult), inc="sDVE")
    P.add("vector", lambda e: e.tensor_scalar(scr[0:1, 7:8], scr[0:1, 0:1],
          0.5, None, ALU.mult), inc="sDVE")
    hwb = bcast_dma(scr[0:1, 7:8], 0, P.val("sDVE"))
    hw_done = P.add("vector", lambda e: e.tensor_copy(mn_a[:, 1:2], sc[:, 0:1]),
                    waits=[("sAUX", hwb)], inc="sDVE")

    if STAGE < 0.199:
        for t in range(NT):
            dma("sync", out_my[128 * t:128 * (t + 1), :],
                xs[:, 1024 * t:1024 * (t + 1)],
                waits=[("sIN", P.val("sIN")), ("sDVE", P.val("sDVE"))])
        P.mark("sync", "sSN", waits=[("sIN", P.val("sIN"))])
        return _finish(nc, P, es)

    lastB = [0, 0]
    last_psT = 0
    last_tr = 0
    prev_qw = 0
    for o in range(8):
        st = fwst if o % 2 == 0 else abst
        d2 = dma("sync", st, fw[128 * o:128 * (o + 1), :],
                 waits=[("sDVE", max(redA[-1], lastB[o % 2]))])
        if True:
            sg_v = P.add("scalar", lambda e, s=st: e.activation(sgst, s, AF.Sign),
                         waits=[("sIN", d2), ("sDVE", max(hw_done, prev_qw))],
                         inc="sACT")
            ab_v = P.add("scalar", lambda e, s=st: e.activation(qf32[0], s, AF.Abs),
                         inc="sACT")
            g_v = P.add("vector", lambda e: e.tensor_scalar(
                qf32[0], qf32[0], sc[:, 0:1], None, ALU.is_gt),
                waits=[("sACT", ab_v)], inc="sDVE")
            qw_v = P.add("vector", lambda e: e.tensor_mul(qb_t[0], sgst, qf32[0]),
                         waits=[("sACT", sg_v), ("sDVE", g_v),
                                ("sPE", last_tr)], inc="sDVE")
        lastB[o % 2] = qw_v
        prev_qw = qw_v
        if STAGE < 0.21:
            continue
        for g in range(2):
            def trfn(e, g=g):
                for j in range(4):
                    e.transpose(psT[:, 128 * j:128 * (j + 1)],
                                qb_t[0][:, (4 * g + j) * 128:(4 * g + j + 1) * 128],
                                identb[:, :])
                return e.drain()
            tr_v = P.add("tensor", trfn,
                         waits=[("sDVE", max(qw_v, last_psT))], inc="sPE")
            last_tr = tr_v
            if True:
                last_psT = P.add("vector", lambda e, g=g, o=o: e.tensor_copy(
                    bass.AP(big, (4 * g) * 1024 + o * 128,
                            [[16384, 128], [1024, 4], [1, 128]]),
                    psT.ap()[:, 0:512].rearrange("p (j c) -> p j c", j=4)),
                    waits=[("sPE", tr_v)], inc="sDVE")

    def ln_stats(minmax):
        for t in range(NT):
            xt = xs[:, 1024 * t:1024 * (t + 1)]
            P.add("vector", lambda e, xt=xt, t=t: e.reduce_sum(
                m_a[:, t:t + 1], xt, AXX), inc="sDVE")
            P.add("vector", lambda e, xt=xt: e.tensor_mul(sqs, xt, xt),
                  inc="sDVE")
            P.add("vector", lambda e, t=t: e.reduce_sum(
                v_a[:, t:t + 1], sqs, AXX), inc="sDVE")
            if minmax:
                P.add("vector", lambda e, xt=xt, t=t: e.reduce_max(
                    mx_a[:, t:t + 1], xt, AXX), inc="sDVE")
                P.add("vector", lambda e, xt=xt, t=t: e.tensor_reduce(
                    mn_a[:, t:t + 1], xt, AXX, ALU.min), inc="sDVE")
        P.add("vector", lambda e: e.tensor_scalar(m_a[:, :], m_a[:, :],
              1.0 / D, None, ALU.mult), inc="sDVE")
        P.add("vector", lambda e: e.tensor_scalar(v_a[:, :], v_a[:, :],
              1.0 / D, None, ALU.mult), inc="sDVE")
        P.add("vector", lambda e: e.tensor_mul(ws_a[:, :], m_a[:, :], m_a[:, :]),
              inc="sDVE")
        P.add("vector", lambda e: e.tensor_sub(v_a[:, :], v_a[:, :], ws_a[:, :]),
              inc="sDVE")
        P.add("vector", lambda e: e.tensor_scalar(v_a[:, :], v_a[:, :],
              EPS, None, ALU.add), inc="sDVE")
        vd = P.val("sDVE")
        P.add("scalar", lambda e: e.activation(ws_a[:, :], v_a[:, :], AF.Ln),
              waits=[("sDVE", vd)], inc="sACT")
        P.add("scalar", lambda e: e.activation(rs_a[:, :], ws_a[:, :], AF.Exp,
              scale=-0.5), inc="sACT")
        return P.val("sACT")

    def iscale_collective():
        rsd = P.val("sACT")
        P.add("vector", lambda e: e.tensor_sub(mx_a[:, :], mx_a[:, :], m_a[:, :]),
              waits=[("sACT", rsd)], inc="sDVE")
        P.add("vector", lambda e: e.tensor_sub(mn_a[:, :], m_a[:, :], mn_a[:, :]),
              inc="sDVE")
        P.add("vector", lambda e: e.tensor_max(mx_a[:, :], mx_a[:, :], mn_a[:, :]),
              inc="sDVE")
        P.add("vector", lambda e: e.tensor_mul(mx_a[:, :], mx_a[:, :], rs_a[:, :]),
              inc="sDVE")
        for j in range(4):
            P.add("vector", lambda e, j=j: e.reduce_max(c4[:, j:j + 1],
                  mx_a[:, :], AXX), inc="sDVE")
        col_reduce_dma(scr[0:1, 1:2], ALU.max, P.val("sDVE"))
        lm = P.add("vector", lambda e: e.tensor_copy(msc[0:1, 0:1],
                   scr[0:1, 1:2]), inc="sDVE")
        d = dma("gpsimd", mx_in[:, :], msc[:, :], waits=[("sDVE", lm)])
        P.add("gpsimd", lambda e: e.collective_compute(
            "AllReduce", ALU.max, replica_groups=[list(range(8))],
            ins=[mx_in.ap().opt()], outs=[mx_out.ap().opt()]),
            waits=[("sAUX", d)], inc="sCC")
        gl = dma("gpsimd", msc[:, :], mx_out[:, :],
                 waits=[("sCC", P.val("sCC"))])
        P.add("vector", lambda e: e.tensor_scalar(scr[0:1, 2:3], msc[0:1, 0:1],
              1.0 / 127.0, None, ALU.mult), waits=[("sAUX", gl)], inc="sDVE")
        P.add("vector", lambda e: e.reciprocal(scr[0:1, 3:4], scr[0:1, 2:3]),
              inc="sDVE")
        P.add("vector", lambda e: e.tensor_mul(scr[0:1, 4:5], scr[0:1, 2:3],
              scr[0:1, 0:1]), inc="sDVE")
        P.add("vector", lambda e: e.tensor_scalar(scr[0:1, 5:6], scr[0:1, 4:5],
              0.5, None, ALU.mult), inc="sDVE")
        P.add("vector", lambda e: e.reciprocal(scr[0:1, 6:7], scr[0:1, 4:5]),
              inc="sDVE")
        P.add("vector", lambda e: e.tensor_scalar(biasq[:, :], fbrow,
              scr[0:1, 6:7], None, ALU.mult), inc="sDVE")
        sd = P.val("sDVE")
        bcast_dma(scr[0:1, 3:4], 1, sd)
        b2 = bcast_dma(scr[0:1, 5:6], 2, sd)
        P.mark("vector", "sDVE", waits=[("sAUX", b2)])
        P.add("vector", lambda e: e.tensor_scalar(rs_a[:, :], rs_a[:, :],
              sc[:, 4:5], None, ALU.mult), inc="sDVE")
        P.add("vector", lambda e: e.tensor_mul(nm_a[:, :], m_a[:, :],
              rs_a[:, :]), inc="sDVE")
        P.add("vector", lambda e: e.tensor_scalar(nm_a[:, :], nm_a[:, :],
              -1.0, None, ALU.mult), inc="sDVE")

    def ffn_pass():
        prev_stt = P.val("sDVE")
        rv_hist = [0, 0]
        psT_ev = P.val("sDVE")
        for t in range(NT):
            par = t % 2
            xt = xs[:, 1024 * t:1024 * (t + 1)]
            if True:
                qv = P.add("scalar", lambda e, xt=xt, t=t, par=par: e.activation(
                    qf32[par], xt, AF.Identity, bias=nm_a[:, t:t + 1],
                    scale=rs_a[:, t:t + 1]), waits=[("sDVE", rv_hist[par])],
                    inc="sACT")
                rv = P.add("vector", lambda e, par=par: e.tensor_scalar(
                    qb_t[par], qf32[par], MAGIC, MAGIC, ALU.add, ALU.subtract),
                    waits=[("sACT", qv)], inc="sDVE")
            rv_hist[par] = rv
            evs = []
            for g in range(2):
                tv = P.add("tensor", lambda e, g=g, par=par: [
                    e.transpose(psT[:, 128 * j:128 * (j + 1)],
                                qb_t[par][:, (4 * g + j) * 128:(4 * g + j + 1) * 128],
                                identb[:, :]) for j in range(4)][-1],
                    waits=[("sDVE", max(rv, psT_ev))], inc="sPE")
                psT_ev = P.add("vector", lambda e, g=g, par=par: e.tensor_copy(
                    qT_t[par][:, 512 * g:512 * (g + 1)], psT[:, 0:512]),
                    waits=[("sPE", tv)], inc="sDVE")
                evs.append(psT_ev)
            stts = [prev_stt]
            for h2 in range(2):
                ps = psC if h2 == 0 else psD
                mm = P.add("tensor", lambda e, par=par, h2=h2, ps=ps: [
                    e.matmul(ps[:, :],
                             qT_t[par][:, 128 * k:128 * (k + 1)],
                             big[:, 1024 * k + 512 * h2:
                                 1024 * k + 512 * (h2 + 1)],
                             start=(k == 0), stop=False) for k in range(8)][-1],
                    waits=[("sDVE", max(evs + [prev_stt]))], inc="sPE")
                mm = P.add("tensor", lambda e, h2=h2, ps=ps: e.matmul(
                    ps[:, :], oneb[:, :], biasq[:, 512 * h2:512 * (h2 + 1)],
                    start=False, stop=True), inc="sPE")
                stts.append(P.add("vector", lambda e, xt=xt, h2=h2, ps=ps:
                    e.scalar_tensor_tensor(
                        xt[:, 512 * h2:512 * (h2 + 1)], ps[:, :], sc[:, 8:9],
                        xt[:, 512 * h2:512 * (h2 + 1)], ALU.mult, ALU.add),
                    waits=[("sPE", mm)], inc="sDVE"))
            prev_stt = max(stts)

    P.barrier()
    if STAGE >= 0.3:
        ln_stats(minmax=True)
        iscale_collective()
        P.barrier()
    if STAGE >= 1:
        ffn_pass()
        P.barrier()

    if STAGE >= 2:
        for k in range(8):
            dma("sync", qwT_dram[:, 1024 * k:1024 * (k + 1)],
                big[:, 1024 * k:1024 * (k + 1)], fence=(k == 7))
        qwT_saved = P.val("sIN")
        ln_stats(minmax=False)
        rsd = P.val("sACT")
        P.add("vector", lambda e: e.tensor_mul(nm_a[:, :], m_a[:, :],
              rs_a[:, :]), waits=[("sACT", rsd)], inc="sDVE")
        st_done = P.add("vector", lambda e: e.tensor_scalar(nm_a[:, :],
                        nm_a[:, :], -1.0, None, ALU.mult), inc="sDVE")
        ln2dma = []
        for t in range(NT):
            par = t % 2
            xt = xs[:, 1024 * t:1024 * (t + 1)]
            w = [("sDVE", st_done)]
            if t >= 2:
                w.append(("sIN", ln2dma[t - 2]))
            av = P.add("scalar", lambda e, xt=xt, t=t, par=par: e.activation(
                ln2st[:, 1024 * par:1024 * (par + 1)], xt, AF.Identity,
                bias=nm_a[:, t:t + 1], scale=rs_a[:, t:t + 1]),
                waits=w, inc="sACT")
            d = dma("sync", ln2_in[128 * t:128 * (t + 1), :],
                    ln2st[:, 1024 * par:1024 * (par + 1)],
                    waits=[("sACT", av)], fence=(t >= NT - 2))
            ln2dma.append(d)
        P.add("gpsimd", lambda e: e.collective_compute(
            "AllGather", ALU.bypass, replica_groups=[list(range(8))],
            ins=[ln2_in.ap().opt()], outs=[ln2_out.ap().opt()]),
            waits=[("sIN", ln2dma[-1]), ("sAUX", P.val("sAUX"))], inc="sCC")
        ag1 = P.val("sCC")
        wic = 0
        for f in range(8):
            d = dma("sync", wist, wi_my[128 * f:128 * (f + 1), :],
                    waits=[("sIN", qwT_saved), ("sDVE", wic)])
            wic = P.add("vector", lambda e, f=f: e.tensor_copy(
                big[:, 1536 * f:1536 * (f + 1)], wist),
                waits=[("sIN", d)], inc="sDVE")
        bd_ = dma("sync", birow, bi_my[:, :], waits=[("sDVE", wic)])
        bcv = P.add("vector", lambda e: e.tensor_copy(birb, birow),
                    waits=[("sIN", bd_)], inc="sDVE")
        btr = P.add("tensor", lambda e: [
            e.transpose(psT[0:128, 2 * j:2 * j + 1],
                        birb[:, 128 * j:128 * (j + 1)],
                        identb[0:1, 0:1]) for j in range(8)][-1],
            waits=[("sDVE", bcv)], inc="sPE")
        P.add("vector", lambda e: e.tensor_copy(
            bcol[:, :], psT.ap()[:, 0:16].rearrange("p (j two) -> p j two",
                                                    two=2)[:, :, 0]),
            waits=[("sPE", btr)], inc="sDVE")
        P.add("vector", lambda e: e.tensor_scalar(bcol[:, 0:4], bcol[:, 0:4],
              0.125, None, ALU.mult), inc="sDVE")
        P.barrier()

        glob_tr = []
        for bd in range(NB):
            for tt in range(4):
                par = tt % 2
                g = 4 * bd + tt
                w = [("sCC", ag1)]
                if g >= 2:
                    w.append(("sPE", glob_tr[g - 2]))
                d = P.add("gpsimd", lambda e, g=g, par=par: e.indirect_dma_start(
                    ln2st[:, 1024 * par:1024 * (par + 1)], None,
                    ln2_out.ap(), bass.IndirectOffsetOnAxis(idxl[:, g:g + 1], 0)),
                    waits=w, inc="sAUX", by=16)
                for gg in range(2):
                    tv = P.add("tensor", lambda e, gg=gg, par=par: [
                        e.transpose(psT[:, 128 * j:128 * (j + 1)],
                                    ln2st[:, 1024 * par + (4 * gg + j) * 128:
                                          1024 * par + (4 * gg + j + 1) * 128],
                                    identb[:, :]) for j in range(4)][-1],
                        waits=[("sAUX", d), ("sDVE", P.val("sDVE"))], inc="sPE")
                    P.add("vector", lambda e, gg=gg, tt=tt: e.tensor_copy(
                        bass.AP(big, 12288 + (4 * gg) * 512 + 128 * tt,
                                [[16384, 128], [512, 4], [1, 128]]),
                        psT.ap()[:, 0:512].rearrange("p (j c) -> p j c", j=4)),
                        waits=[("sPE", tv)], inc="sDVE")
                glob_tr.append(P.val("sPE"))
            lnt_done = P.val("sDVE")
            for oc in range(8):
                ps = psC if oc % 2 == 0 else psD
                mm = P.add("tensor", lambda e, oc=oc, ps=ps: [
                    e.matmul(ps[:, :],
                             big[:, 1536 * f + 128 * oc:1536 * f + 128 * (oc + 1)],
                             big[:, 12288 + 512 * f:12288 + 512 * (f + 1)],
                             start=(f == 0), stop=(f == 7)) for f in range(8)][-1],
                    waits=[("sDVE", lnt_done), ("sACT", P.val("sACT"))],
                    inc="sPE")
                isq = oc < 4
                dest = (qt if isq else kt)
                doff = (oc if isq else oc - 4) * 4096 + 512 * bd
                P.add("scalar", lambda e, ps=ps, dest=dest, doff=doff, oc=oc,
                      isq=isq: e.activation(
                          dest[:, doff:doff + 512], ps[:, :], AF.Identity,
                          bias=bcol[:, oc:oc + 1], scale=(0.125 if isq else 1.0)),
                      waits=[("sPE", mm)], inc="sACT")
            for tt in range(4):
                ps = psE if tt % 2 == 0 else psF
                kc = 4 * bd + tt
                mm = P.add("tensor", lambda e, tt=tt, ps=ps: [
                    e.matmul(ps[:, :],
                             big[:, 12288 + 512 * f + 128 * tt:
                                  12288 + 512 * f + 128 * (tt + 1)],
                             big[:, 1536 * f + 1024:1536 * (f + 1)],
                             start=(f == 0), stop=False) for f in range(8)][-1],
                    waits=[("sDVE", P.val("sDVE"))], inc="sPE")
                mm = P.add("tensor", lambda e, ps=ps: e.matmul(
                    ps[:, :], oneb[:, :], bvrow, start=False, stop=True),
                    inc="sPE")
                P.add("vector", lambda e, ps=ps, kc=kc: e.tensor_copy(
                    bass.AP(vb, 65 * kc, [[8 * 32 * 65, 128], [2080, 8], [1, 64]]),
                    ps.ap().rearrange("p (j c) -> p j c", j=8)),
                    waits=[("sPE", mm)], inc="sDVE")
        P.barrier()

        for pp in range(4):
            for bd in range(NB):
                kmax = 4 * bd + 4
                qbase = 4096 * pp + 512 * bd
                exp_done = [0] * (kmax + 1)
                pool_done = [0] * (kmax + 1)
                av_done = [0, 0]
                for kc in range(kmax):
                    j0 = max(0, kc - 4 * bd)
                    qo = 128 * j0
                    par = kc % 2
                    psCk = psC if par == 0 else psA.ap()[:, 0:512]
                    psDk = psD if par == 0 else psA.ap()[:, 512:1024]
                    eA = expb[:, 0:512]
                    eB = expb[:, 512:1024]
                    w = [("sACT", exp_done[kc - 2] if kc >= 2 else 0),
                         ("sPE", av_done[0])]
                    sA = P.add("tensor", lambda e, pp=pp, kc=kc, qo=qo,
                               qbase=qbase, psCk=psCk: e.matmul(
                                   psCk[:, qo:512],
                                   kt[0:64, 4096 * pp + 128 * kc:
                                      4096 * pp + 128 * (kc + 1)],
                                   qt[0:64, qbase + qo:qbase + 512],
                                   start=True, stop=True), waits=w, inc="sPE")
                    sB = P.add("tensor", lambda e, pp=pp, kc=kc, qo=qo,
                               qbase=qbase, psDk=psDk: e.matmul(
                                   psDk[:, qo:512],
                                   kt[64:128, 4096 * pp + 128 * kc:
                                      4096 * pp + 128 * (kc + 1)],
                                   qt[64:128, qbase + qo:qbase + 512],
                                   start=True, stop=True,
                                   tile_position=(64, 0)), inc="sPE")
                    we = [("sPE", sA)]
                    if kc >= 2:
                        we.append(("sPOOL", pool_done[kc - 2]))
                    pspair = psCD if par == 0 else psA
                    ed = P.add("scalar", lambda e, qo=qo, pspair=pspair:
                               e.activation(
                                   bass.AP(expb, qo,
                                           [[1024, 128], [512, 2],
                                            [1, 512 - qo]]),
                                   bass.AP(pspair, qo,
                                           [[1024, 128], [512, 2],
                                            [1, 512 - qo]]),
                                   AF.Exp),
                               waits=we + [("sPE", sB)], inc="sACT")
                    exp_done[kc] = ed
                    wv = [("sACT", ed)]
                    if kc >= 4 * bd:
                        P.add("gpsimd", lambda e, eA=eA, qo=qo: e.affine_select(
                            eA[:, qo:qo + 128], eA[:, qo:qo + 128], [[1, 128]],
                            ALU.is_ge, 0.0, base=0, channel_multiplier=-1),
                            waits=[("sACT", ed)], inc="sPOOL")
                        pd = P.add("gpsimd", lambda e, eB=eB, qo=qo:
                                   e.affine_select(
                                       eB[:, qo:qo + 128], eB[:, qo:qo + 128],
                                       [[1, 128]], ALU.is_ge, 0.0, base=0,
                                       channel_multiplier=-1), inc="sPOOL")
                        pool_done[kc] = pd
                        wv.append(("sPOOL", pd))
                    else:
                        pool_done[kc] = pool_done[kc - 1] if kc else 0
                    P.add("tensor", lambda e, pp=pp, kc=kc, qo=qo, eA=eA,
                          kmax=kmax: e.matmul(
                              psE[0:65, qo:512],
                              vb[:, 2080 * (2 * pp) + 65 * kc:
                                 2080 * (2 * pp) + 65 * kc + 65],
                              eA[:, qo:512],
                              start=(kc == 0), stop=(kc == kmax - 1)),
                          waits=wv, inc="sPE")
                    av_done[par] = P.add("tensor", lambda e, pp=pp, kc=kc,
                        qo=qo, eB=eB, kmax=kmax: e.matmul(
                            psF[0:65, qo:512],
                            vb[:, 2080 * (2 * pp + 1) + 65 * kc:
                               2080 * (2 * pp + 1) + 65 * kc + 65],
                            eB[:, qo:512],
                            start=(kc == 0), stop=(kc == kmax - 1)),
                        inc="sPE")
                last = P.val("sPE")

                def lp(fn):
                    def g(e):
                        with nc.allow_low_precision("softmax denom in bf16"):
                            return fn(e)
                    return g
                P.add("vector", lp(lambda e: e.reciprocal(rD[0:1, 0:512],
                      psE[64:65, :])), waits=[("sPE", last)], inc="sDVE")
                r2 = P.add("vector", lp(lambda e: e.reciprocal(
                    rD[0:1, 512:1024], psF[64:65, :])), inc="sDVE")
                P.add("tensor", lambda e: e.matmul(
                    psC[0:64, :], oneb[0:1, 0:64], rD[0:1, 0:512],
                    start=True, stop=True),
                    waits=[("sDVE", r2), ("sACT", P.val("sACT"))], inc="sPE")
                b2 = P.add("tensor", lambda e: e.matmul(
                    psC[64:128, :], oneb[0:1, 0:64], rD[0:1, 512:1024],
                    start=True, stop=True, tile_position=(0, 64)), inc="sPE")
                rbv = P.add("vector", lambda e: e.tensor_copy(rb[:, :],
                            psC[:, :]), waits=[("sPE", b2)], inc="sDVE")
                P.add("vector", lambda e, qbase=qbase: e.tensor_mul(
                    big[0:64, qbase:qbase + 512], psE[0:64, :], rb[0:64, :]),
                    waits=[("sDVE", rbv)], inc="sDVE")
                n2 = P.add("vector", lambda e, qbase=qbase: e.tensor_mul(
                    big[64:128, qbase:qbase + 512], psF[0:64, :],
                    rb[64:128, :]), inc="sDVE")
                P.pend("tensor", "sDVE", n2)
        P.barrier()

        for p4 in range(4):
            for hh in range(2):
                dma("sync", ot_in[128 * (4 * hh + p4):128 * (4 * hh + p4 + 1), :],
                    big[:, 4096 * p4 + 2048 * hh:4096 * p4 + 2048 * (hh + 1)],
                    fence=(p4 == 3 and hh == 1))
        otd = P.val("sIN")
        P.add("gpsimd", lambda e: e.collective_compute(
            "AllGather", ALU.bypass, replica_groups=[list(range(8))],
            ins=[ot_in.ap().opt()], outs=[ot_out.ap().opt()]),
            waits=[("sIN", otd), ("sAUX", P.val("sAUX"))], inc="sCC")
        ag2 = P.val("sCC")
        got = 0
        for g in range(8):
            dest = kt if g < 4 else qt
            doff = 2048 * (g % 4)
            got = P.add("gpsimd", lambda e, g=g, dest=dest, doff=doff:
                        e.indirect_dma_start(
                            dest[:, doff:doff + 2048], None, ot_out.ap(),
                            bass.IndirectOffsetOnAxis(idxo[:, g:g + 1], 0)),
                        waits=[("sCC", ag2)], inc="sAUX", by=16)
            if g == 7:
                P.pend("gpsimd", "sAUX", got)
        woc = 0
        for j in range(8):
            d = dma("sync", worow, wo_cat[128 * j:128 * (j + 1), :],
                    waits=[("sIN", otd), ("sDVE", woc)])
            woc = P.add("vector", lambda e, j=j: e.tensor_copy(
                qt[:, 8192 + 1024 * j:8192 + 1024 * (j + 1)], worow),
                waits=[("sIN", d)], inc="sDVE")
        bod = dma("sync", worow[0:1, :], bo[:, :], waits=[("sDVE", woc)])
        bocast = P.add("vector", lambda e: e.tensor_copy(borow[:, :],
                       worow[0:1, :]), waits=[("sIN", bod)], inc="sDVE")
        P.barrier()

        prev = max(P.val("sDVE"), bocast)
        for t in range(NT):
            mm = P.add("tensor", lambda e, t=t: [
                e.matmul(psA[:, 512 * h2:512 * (h2 + 1)],
                         (kt if j < 4 else qt)[:, 2048 * (j % 4) + 128 * t:
                                               2048 * (j % 4) + 128 * (t + 1)],
                         qt[:, 8192 + 1024 * j + 512 * h2:
                            8192 + 1024 * j + 512 * (h2 + 1)],
                         start=(j == 0), stop=False)
                for j in range(8) for h2 in range(2)][-1],
                waits=[("sAUX", got), ("sDVE", prev)], inc="sPE")
            mm = P.add("tensor", lambda e: [
                e.matmul(psA[:, 512 * h2:512 * (h2 + 1)], oneb[:, :],
                         borow[:, 512 * h2:512 * (h2 + 1)],
                         start=False, stop=True) for h2 in range(2)][-1],
                inc="sPE")
            prev = P.add("vector", lambda e, t=t: [
                e.tensor_add(
                    xs[:, 1024 * t + 512 * h2:1024 * t + 512 * (h2 + 1)],
                    xs[:, 1024 * t + 512 * h2:1024 * t + 512 * (h2 + 1)],
                    psA[:, 512 * h2:512 * (h2 + 1)]) for h2 in range(2)][-1],
                waits=[("sPE", mm)], inc="sDVE")
        P.barrier()

    if STAGE >= 3:
        for k in range(8):
            dma("sync", big[:, 1024 * k:1024 * (k + 1)],
                qwT_dram[:, 1024 * k:1024 * (k + 1)], fence=(k == 7))
        qw2 = dma("sync", fbrow, fb[:, :])
        ln_stats(minmax=True)
        iscale_collective()
        P.barrier()
        P.pend("tensor", "sIN", qw2)
        ffn_pass()
        P.barrier()

    for t in range(NT):
        dma("sync", out_my[128 * t:128 * (t + 1), :],
            xs[:, 1024 * t:1024 * (t + 1)], fence=False)
    P.mark("sync", "sSN", waits=[("sIN", P.val("sIN"))])

    return _finish(nc, P, es)


def _finish(nc, P, es):
    with (
        nc.semaphore("sIN") as sIN, nc.semaphore("sAUX") as sAUX,
        nc.semaphore("sCC") as sCC, nc.semaphore("sPE") as sPE,
        nc.semaphore("sDVE") as sDVE, nc.semaphore("sACT") as sACT,
        nc.semaphore("sPOOL") as sPOOL, nc.semaphore("sSN") as sSN,
        nc.semaphore("sGN") as sGN,
        nc.Block() as block,
    ):
        sems = dict(sIN=sIN, sAUX=sAUX, sCC=sCC, sPE=sPE, sDVE=sDVE,
                    sACT=sACT, sPOOL=sPOOL, sSN=sSN, sGN=sGN)
        P.emit(block, sems)
    es.close()
    return nc


_NC_CACHE = {}


def kernel(L0_attn_x, L0_attn_w_in, L0_attn_b_in, L0_attn_w_out, L0_attn_b_out,
           L0_ffn_weight, L0_ffn_bias):
    x = np.ascontiguousarray(np.asarray(L0_attn_x, dtype=np.float32))
    w_in = np.ascontiguousarray(np.asarray(L0_attn_w_in, dtype=np.float32))
    b_in = np.asarray(L0_attn_b_in, dtype=np.float32).reshape(1, 3072)
    w_out = np.ascontiguousarray(np.asarray(L0_attn_w_out, dtype=np.float32))
    b_out = np.asarray(L0_attn_b_out, dtype=np.float32).reshape(1, D)
    fw = np.ascontiguousarray(np.asarray(L0_ffn_weight, dtype=np.float32))
    fb = np.asarray(L0_ffn_bias, dtype=np.float32).reshape(1, D)

    in_maps = []
    for c in range(8):
        b, h = c // 2, c % 2
        qs = slice(512 * h, 512 * h + 512)
        ks = slice(1024 + 512 * h, 1024 + 512 * h + 512)
        vs = slice(2048 + 512 * h, 2048 + 512 * h + 512)
        wi = np.ascontiguousarray(
            np.concatenate([w_in[:, qs], w_in[:, ks], w_in[:, vs]], axis=1))
        bi = np.ascontiguousarray(
            np.concatenate([b_in[:, qs], b_in[:, ks], b_in[:, vs]], axis=1))
        wo = np.ascontiguousarray(np.concatenate(
            [w_out[512 * h:512 * h + 512, :],
             w_out[512 * (1 - h):512 * (1 - h) + 512, :]], axis=0))
        idxl_np = (b * 4096 + np.arange(4096, dtype=np.uint32)
                   ).reshape(32, 128).T.copy()
        rows = np.concatenate([
            c * 1024 + 512 * h + np.arange(512, dtype=np.uint32),
            (c ^ 1) * 1024 + 512 * h + np.arange(512, dtype=np.uint32)])
        idxo_np = rows.reshape(8, 128).T.copy()
        in_maps.append(dict(
            x_my=np.ascontiguousarray(x[b, 2048 * h:2048 * h + 2048, :]),
            wi_my=wi, bi_my=bi, wo_cat=wo, bo=b_out,
            fw=fw, fb=fb, idx_ln2=idxl_np, idx_ot=idxo_np))

    if "nc" not in _NC_CACHE:
        _NC_CACHE["nc"] = build_nc()
    res = run_bass_kernel_spmd(_NC_CACHE["nc"], in_maps, core_ids=list(range(8)))
    out = np.empty((B, S, D), dtype=np.float32)
    for c in range(8):
        b, h = c // 2, c % 2
        out[b, 2048 * h:2048 * h + 2048, :] = res.results[c]["out_my"]
    return out


# version 07
# speedup vs baseline: 1.1797x; 1.1108x over previous
import numpy as np
import ml_dtypes
import concourse.bass as bass
import concourse.mybir as mybir
from concourse.bass_utils import run_bass_kernel_spmd

F32 = mybir.dt.float32
BF16 = mybir.dt.bfloat16
U32 = mybir.dt.uint32
AF = mybir.ActivationFunctionType
ALU = mybir.AluOpType
AXX = mybir.AxisListType.X

B, S, D = 4, 4096, 1024
TOK = 2048
NT = TOK // 128
NB = S // 512
MAGIC = 1.5 * 2.0 ** 23
EPS = 1e-5
STAGE = 3
RACE_DETECT = False

ENGINES = ("sync", "gpsimd", "tensor", "vector", "scalar")
ESEM = {"sync": "sSN", "gpsimd": "sGN", "tensor": "sPE",
        "vector": "sDVE", "scalar": "sACT"}


class Prog:

    def __init__(self):
        self.ops = {e: [] for e in ENGINES}
        self.c = {}
        self.pending = {e: {} for e in ENGINES}
        self.last = {}

    def val(self, sem):
        return self.c.get(sem, 0)

    SERIAL = ("vector", "scalar")

    def add(self, eng, fn, waits=(), inc=None, by=1, selfinc=False):
        w = dict(self.pending[eng])
        self.pending[eng] = {}
        for s, v in waits:
            w[s] = max(w.get(s, 0), v)
        if eng in self.SERIAL:
            prev = self.last.get(eng)
            if prev is not None:
                w[prev[0]] = max(w.get(prev[0], 0), prev[1])
        if inc is not None:
            self.c[inc] = self.c.get(inc, 0) + by
            if eng in self.SERIAL:
                self.last[eng] = (inc, self.c[inc])
        self.ops[eng].append((fn, w, None if selfinc else inc, by))
        return self.c.get(inc) if inc is not None else None

    tensor_mark_fn = None

    def mark(self, eng, sem, waits=()):
        if eng == "tensor":
            return self.add(eng, self.tensor_mark_fn, waits=waits, inc=sem)
        return self.add(eng, lambda en, s=sem: en.sem_inc(self._sems[s], 1),
                        waits=waits, inc=sem, selfinc=True)

    def pend(self, eng, sem, v):
        self.pending[eng][sem] = max(self.pending[eng].get(sem, 0), v)

    def barrier(self):
        marks = []
        for e in ENGINES:
            w = []
            if e == "sync":
                w = [("sIN", self.val("sIN"))]
            elif e == "gpsimd":
                w = [("sAUX", self.val("sAUX")), ("sCC", self.val("sCC"))]
            elif e == "tensor":
                w = [("sDVE", self.val("sDVE")), ("sACT", self.val("sACT"))]
            v = self.mark(e, ESEM[e], waits=w)
            marks.append((ESEM[e], v))
        for e in ENGINES:
            for s, v in marks:
                self.pend(e, s, v)

    def emit(self, block, sems):
        self._sems = sems
        def mk(engname):
            def run(eng):
                seen = {}
                for fn, waits, inc, by in self.ops[engname]:
                    for s, v in waits.items():
                        if v > 0 and seen.get(s, -1) < v:
                            eng.wait_ge(sems[s], v)
                            seen[s] = v
                    ins = fn(eng)
                    if inc is not None:
                        ins.then_inc(sems[inc], by)
            return run
        block.sync(mk("sync"))
        block.gpsimd(mk("gpsimd"))
        block.tensor(mk("tensor"))
        block.vector(mk("vector"))
        block.scalar(mk("scalar"))


def build_nc():
    nc = bass.Bass("TRN2", target_bir_lowering=False,
                   detect_race_conditions=RACE_DETECT)

    x_my = nc.dram_tensor("x_my", [TOK, D], F32, kind="ExternalInput")
    wi_my = nc.dram_tensor("wi_my", [D, 1536], F32, kind="ExternalInput")
    bi_my = nc.dram_tensor("bi_my", [1, 1536], F32, kind="ExternalInput")
    wo_cat = nc.dram_tensor("wo_cat", [D, D], F32, kind="ExternalInput")
    bo = nc.dram_tensor("bo", [1, D], F32, kind="ExternalInput")
    fw = nc.dram_tensor("fw", [D, D], F32, kind="ExternalInput")
    fb = nc.dram_tensor("fb", [1, D], F32, kind="ExternalInput")
    idx_ln2 = nc.dram_tensor("idx_ln2", [128, 32], U32, kind="ExternalInput")
    idx_ot = nc.dram_tensor("idx_ot", [128, 8], U32, kind="ExternalInput")
    out_my = nc.dram_tensor("out_my", [TOK, D], F32, kind="ExternalOutput")

    qwT_dram = nc.dram_tensor("qwT_dram", [128, 8192], BF16, kind="Internal")
    ln2_in = nc.dram_tensor("ln2_in", [TOK, D], BF16, kind="Internal")
    ln2_out = nc.dram_tensor("ln2_out", [2 * TOK, D], BF16, kind="Internal")
    ot_in = nc.dram_tensor("ot_in", [1024, TOK], BF16, kind="Internal")
    ot_out = nc.dram_tensor("ot_out", [2048, TOK], BF16, kind="Internal")
    sc_dram = nc.dram_tensor("sc_dram", [1, 512], F32, kind="Internal")
    mx_in = nc.dram_tensor("mx_in", [1, 16], F32, kind="Internal")
    mx_out = nc.dram_tensor("mx_out", [1, 16], F32, kind="Internal")

    import contextlib
    es = contextlib.ExitStack()
    SB = lambda nm, sh, dt: es.enter_context(nc.sbuf_tensor(nm, sh, dt))
    PS = lambda nm, sh, dt: es.enter_context(nc.psum_tensor(nm, sh, dt))

    xs = SB("xs", [128, NT * 1024], F32)
    qt = SB("qt", [128, 16384], BF16)
    kt = SB("kt", [128, 16384], BF16)
    vb = SB("vb", [128, 8 * 32 * 65], BF16)
    big = SB("big", [128, 16384], BF16)
    identf = SB("identf", [128, 128], F32)
    identb = SB("identb", [128, 128], BF16)
    ln2st = SB("ln2st", [128, 2 * 1024], BF16)
    expb = SB("expb", [128, 2 * 512], BF16)
    rb = SB("rb", [128, 512], BF16)
    m_a = SB("m_a", [128, 16], F32)
    v_a = SB("v_a", [128, 16], F32)
    rs_a = SB("rs_a", [128, 16], F32)
    nm_a = SB("nm_a", [128, 16], F32)
    mx_a = SB("mx_a", [128, 16], F32)
    mn_a = SB("mn_a", [128, 16], F32)
    ws_a = SB("ws_a", [128, 16], F32)
    sc = SB("sc", [128, 12], F32)
    c4 = SB("c4", [128, 4], F32)
    bcol = SB("bcol", [128, 8], F32)
    scr = SB("scr", [1, 16], F32)
    msc = SB("msc", [1, 16], F32)
    onef = SB("onef", [1, 128], F32)
    oneb = SB("oneb", [1, 128], BF16)
    onec = SB("onec", [128, 1], F32)
    rowt = SB("rowt", [1, 512], F32)
    rowb = SB("rowb", [1, 1536], BF16)
    biasq = rowb.ap()[0:1, 0:1024]
    birb = rowb.ap()
    bvrow = rowb.ap()[0:1, 1024:1536]
    rD = rowb.ap()[0:1, 0:1024]
    borow = rowb.ap()[0:1, 0:1024]
    idxl = SB("idxl", [128, 32], U32)
    idxo = SB("idxo", [128, 8], U32)

    psA = PS("psA", [128, 1024], F32)
    psCD = PS("psCD", [128, 1024], F32)
    psC = psCD.ap()[:, 0:512]
    psD = psCD.ap()[:, 512:1024]
    psE = PS("psE", [128, 512], F32)
    psF = PS("psF", [128, 512], F32)
    psT = PS("psT", [128, 1024], BF16)

    qtf = qt.ap().bitcast(F32)
    qf32 = [qtf[:, 0:1024], qtf[:, 1024:2048]]
    sqs = qtf[:, 1024:2048]
    fwst = qtf[:, 2048:3072]
    abst = qtf[:, 3072:4096]
    sgst = qtf[:, 4096:5120]
    wist = qtf[:, 5120:6656]
    birow = qtf[0:1, 5120:6656]
    fbrow = qtf[0:1, 6656:7680]
    worow = big.ap().bitcast(F32)[:, 0:1024]
    ktb = kt.ap()
    qb_t = [ktb[:, 0:1024], ktb[:, 1024:2048]]
    qT_t = [ktb[:, 2048:3072], ktb[:, 3072:4096]]

    P = Prog()
    P.tensor_mark_fn = lambda e: e.transpose(psT[:, 0:128], identb[:, :],
                                             identb[:, :])

    def dma(eng_name, dst, src, waits=(), sem=None, slow=False, fence=True):
        sem = sem or ("sIN" if eng_name == "sync" else "sAUX")
        v = P.add(eng_name, lambda e, d=dst, s=src, sl=slow: e.dma_start(
            out=d, in_=s, allow_slow_non_contiguous=sl), waits=waits,
            inc=sem, by=16)
        if fence:
            P.pend(eng_name, sem, v)
        return v

    def col_reduce_dma(out_cell, op, wait_dve):
        d1 = dma("gpsimd", sc_dram.ap().rearrange("one (p c) -> (one p) c", c=4),
                 c4[:, :], waits=[("sDVE", wait_dve + 0),
                                  ("sAUX", P.val("sAUX")),
                                  ("sDVE", P.val("sDVE"))])
        d2 = dma("gpsimd", rowt[:, :], sc_dram[:, :], waits=[("sAUX", d1)])
        return P.add("vector", lambda e: e.tensor_reduce(
            out_cell, bass.AP(rowt, 0, [[512, 1], [4, 128]]), AXX, op),
            waits=[("sAUX", d2)], inc="sDVE")

    def bcast_dma(cell, slot, wait_dve):
        m = P.add("vector", lambda e: e.memset(rowt[:, :], 1.0),
                  waits=[("sDVE", wait_dve), ("sAUX", P.val("sAUX"))],
                  inc="sDVE")
        r = P.add("vector", lambda e: e.tensor_scalar(
            rowt[:, :], rowt[:, :], cell, None, ALU.mult), inc="sDVE")
        d1 = dma("gpsimd", sc_dram[:, :], rowt[:, :],
                 waits=[("sDVE", r), ("sAUX", P.val("sAUX"))])
        d2 = dma("gpsimd", sc[:, 4 * slot:4 * slot + 4],
                 sc_dram.ap().rearrange("one (p c) -> (one p) c", c=4),
                 waits=[("sAUX", d1)])
        return d2

    for t in range(NT):
        dma("sync", xs[:, 1024 * t:1024 * (t + 1)],
            x_my[128 * t:128 * (t + 1), :], fence=(t == NT - 1))
    dma("sync", fbrow, fb[:, :])
    dma("sync", idxl[:, :], idx_ln2[:, :])
    dma("sync", idxo[:, :], idx_ot[:, :])

    if STAGE < 0.15:
        for t in range(NT):
            dma("sync", out_my[128 * t:128 * (t + 1), :],
                xs[:, 1024 * t:1024 * (t + 1)],
                waits=[("sIN", P.val("sIN"))])
        P.mark("sync", "sSN", waits=[("sIN", P.val("sIN"))])
        return _finish(nc, P, es)

    P.add("gpsimd", lambda e: e.memset(identf[:, :], 1.0))
    P.add("gpsimd", lambda e: e.affine_select(identf[:, :], identf[:, :],
          [[-1, 128]], ALU.is_equal, 0.0, base=0, channel_multiplier=1))
    P.add("gpsimd", lambda e: e.memset(onef[:, :], 1.0))
    P.add("gpsimd", lambda e: e.memset(oneb[:, :], 1.0))
    P.add("gpsimd", lambda e: e.memset(onec[:, :], 1.0))
    P.add("gpsimd", lambda e: e.memset(msc[:, :], 0.0))
    P.add("gpsimd", lambda e: e.memset(vb[:, :], 1.0), inc="sPOOL")
    pool_const = P.val("sPOOL")
    P.add("vector", lambda e: e.tensor_copy(identb[:, :], identf[:, :]),
          waits=[("sPOOL", pool_const)], inc="sDVE")

    if STAGE < 0.19:
        for t in range(NT):
            dma("sync", out_my[128 * t:128 * (t + 1), :],
                xs[:, 1024 * t:1024 * (t + 1)],
                waits=[("sIN", P.val("sIN")), ("sDVE", P.val("sDVE"))])
        P.mark("sync", "sSN", waits=[("sIN", P.val("sIN"))])
        return _finish(nc, P, es)

    redA = []
    for o in range(8):
        st = fwst if o % 2 == 0 else abst
        w = [("sDVE", redA[o - 2])] if o >= 2 else ()
        d = dma("sync", st, fw[128 * o:128 * (o + 1), :], waits=w)
        r = P.add("vector", lambda e, s=st, o=o: e.reduce_sum(
            ws_a[:, o:o + 1], s, AXX, apply_absolute_value=True),
            waits=[("sIN", d)], inc="sDVE")
        redA.append(r)
    for j in range(4):
        P.add("vector", lambda e, j=j: e.reduce_sum(c4[:, j:j + 1],
              ws_a[:, 0:8], AXX), inc="sDVE")
    col_reduce_dma(scr[0:1, 0:1], ALU.add, P.val("sDVE"))
    P.add("vector", lambda e: e.tensor_scalar(scr[0:1, 0:1], scr[0:1, 0:1],
          1.0 / (1 << 20), None, ALU.mult), inc="sDVE")
    P.add("vector", lambda e: e.tensor_scalar(scr[0:1, 7:8], scr[0:1, 0:1],
          0.5, None, ALU.mult), inc="sDVE")
    hwb = bcast_dma(scr[0:1, 7:8], 0, P.val("sDVE"))
    hw_done = P.add("vector", lambda e: e.tensor_copy(mn_a[:, 1:2], sc[:, 0:1]),
                    waits=[("sAUX", hwb)], inc="sDVE")

    if STAGE < 0.199:
        for t in range(NT):
            dma("sync", out_my[128 * t:128 * (t + 1), :],
                xs[:, 1024 * t:1024 * (t + 1)],
                waits=[("sIN", P.val("sIN")), ("sDVE", P.val("sDVE"))])
        P.mark("sync", "sSN", waits=[("sIN", P.val("sIN"))])
        return _finish(nc, P, es)

    lastB = [0, 0]
    last_psT = 0
    last_tr = 0
    prev_qw = 0
    for o in range(8):
        st = fwst if o % 2 == 0 else abst
        d2 = dma("sync", st, fw[128 * o:128 * (o + 1), :],
                 waits=[("sDVE", max(redA[-1], lastB[o % 2]))])
        if True:
            sg_v = P.add("scalar", lambda e, s=st: e.activation(sgst, s, AF.Sign),
                         waits=[("sIN", d2), ("sDVE", max(hw_done, prev_qw))],
                         inc="sACT")
            ab_v = P.add("scalar", lambda e, s=st: e.activation(qf32[0], s, AF.Abs),
                         inc="sACT")
            g_v = P.add("vector", lambda e: e.tensor_scalar(
                qf32[0], qf32[0], sc[:, 0:1], None, ALU.is_gt),
                waits=[("sACT", ab_v)], inc="sDVE")
            qw_v = P.add("vector", lambda e: e.tensor_mul(qb_t[0], sgst, qf32[0]),
                         waits=[("sACT", sg_v), ("sDVE", g_v),
                                ("sPE", last_tr)], inc="sDVE")
        lastB[o % 2] = qw_v
        prev_qw = qw_v
        if STAGE < 0.21:
            continue
        for g in range(2):
            def trfn(e, g=g):
                for j in range(4):
                    e.transpose(psT[:, 128 * j:128 * (j + 1)],
                                qb_t[0][:, (4 * g + j) * 128:(4 * g + j + 1) * 128],
                                identb[:, :])
                return e.drain()
            tr_v = P.add("tensor", trfn,
                         waits=[("sDVE", max(qw_v, last_psT))], inc="sPE")
            last_tr = tr_v
            if True:
                last_psT = P.add("vector", lambda e, g=g, o=o: e.tensor_copy(
                    bass.AP(big, (4 * g) * 1024 + o * 128,
                            [[16384, 128], [1024, 4], [1, 128]]),
                    psT.ap()[:, 0:512].rearrange("p (j c) -> p j c", j=4)),
                    waits=[("sPE", tr_v)], inc="sDVE")

    def ln_stats(minmax):
        for t in range(NT):
            xt = xs[:, 1024 * t:1024 * (t + 1)]
            P.add("vector", lambda e, xt=xt, t=t: e.reduce_sum(
                m_a[:, t:t + 1], xt, AXX), inc="sDVE")
            P.add("vector", lambda e, xt=xt: e.tensor_mul(sqs, xt, xt),
                  inc="sDVE")
            P.add("vector", lambda e, t=t: e.reduce_sum(
                v_a[:, t:t + 1], sqs, AXX), inc="sDVE")
            if minmax:
                P.add("vector", lambda e, xt=xt, t=t: e.reduce_max(
                    mx_a[:, t:t + 1], xt, AXX), inc="sDVE")
                P.add("vector", lambda e, xt=xt, t=t: e.tensor_reduce(
                    mn_a[:, t:t + 1], xt, AXX, ALU.min), inc="sDVE")
        P.add("vector", lambda e: e.tensor_scalar(m_a[:, :], m_a[:, :],
              1.0 / D, None, ALU.mult), inc="sDVE")
        P.add("vector", lambda e: e.tensor_scalar(v_a[:, :], v_a[:, :],
              1.0 / D, None, ALU.mult), inc="sDVE")
        P.add("vector", lambda e: e.tensor_mul(ws_a[:, :], m_a[:, :], m_a[:, :]),
              inc="sDVE")
        P.add("vector", lambda e: e.tensor_sub(v_a[:, :], v_a[:, :], ws_a[:, :]),
              inc="sDVE")
        P.add("vector", lambda e: e.tensor_scalar(v_a[:, :], v_a[:, :],
              EPS, None, ALU.add), inc="sDVE")
        vd = P.val("sDVE")
        P.add("scalar", lambda e: e.activation(ws_a[:, :], v_a[:, :], AF.Ln),
              waits=[("sDVE", vd)], inc="sACT")
        P.add("scalar", lambda e: e.activation(rs_a[:, :], ws_a[:, :], AF.Exp,
              scale=-0.5), inc="sACT")
        return P.val("sACT")

    def iscale_collective():
        rsd = P.val("sACT")
        P.add("vector", lambda e: e.tensor_sub(mx_a[:, :], mx_a[:, :], m_a[:, :]),
              waits=[("sACT", rsd)], inc="sDVE")
        P.add("vector", lambda e: e.tensor_sub(mn_a[:, :], m_a[:, :], mn_a[:, :]),
              inc="sDVE")
        P.add("vector", lambda e: e.tensor_max(mx_a[:, :], mx_a[:, :], mn_a[:, :]),
              inc="sDVE")
        P.add("vector", lambda e: e.tensor_mul(mx_a[:, :], mx_a[:, :], rs_a[:, :]),
              inc="sDVE")
        for j in range(4):
            P.add("vector", lambda e, j=j: e.reduce_max(c4[:, j:j + 1],
                  mx_a[:, :], AXX), inc="sDVE")
        col_reduce_dma(scr[0:1, 1:2], ALU.max, P.val("sDVE"))
        lm = P.add("vector", lambda e: e.tensor_copy(msc[0:1, 0:1],
                   scr[0:1, 1:2]), inc="sDVE")
        d = dma("gpsimd", mx_in[:, :], msc[:, :], waits=[("sDVE", lm)])
        P.add("gpsimd", lambda e: e.collective_compute(
            "AllReduce", ALU.max, replica_groups=[list(range(8))],
            ins=[mx_in.ap().opt()], outs=[mx_out.ap().opt()]),
            waits=[("sAUX", d)], inc="sCC")
        gl = dma("gpsimd", msc[:, :], mx_out[:, :],
                 waits=[("sCC", P.val("sCC"))])
        P.add("vector", lambda e: e.tensor_scalar(scr[0:1, 2:3], msc[0:1, 0:1],
              1.0 / 127.0, None, ALU.mult), waits=[("sAUX", gl)], inc="sDVE")
        P.add("vector", lambda e: e.reciprocal(scr[0:1, 3:4], scr[0:1, 2:3]),
              inc="sDVE")
        P.add("vector", lambda e: e.tensor_mul(scr[0:1, 4:5], scr[0:1, 2:3],
              scr[0:1, 0:1]), inc="sDVE")
        P.add("vector", lambda e: e.tensor_scalar(scr[0:1, 5:6], scr[0:1, 4:5],
              0.5, None, ALU.mult), inc="sDVE")
        P.add("vector", lambda e: e.reciprocal(scr[0:1, 6:7], scr[0:1, 4:5]),
              inc="sDVE")
        P.add("vector", lambda e: e.tensor_scalar(biasq[:, :], fbrow,
              scr[0:1, 6:7], None, ALU.mult), inc="sDVE")
        sd = P.val("sDVE")
        bcast_dma(scr[0:1, 3:4], 1, sd)
        b2 = bcast_dma(scr[0:1, 5:6], 2, sd)
        P.mark("vector", "sDVE", waits=[("sAUX", b2)])
        P.add("vector", lambda e: e.tensor_scalar(rs_a[:, :], rs_a[:, :],
              sc[:, 4:5], None, ALU.mult), inc="sDVE")
        P.add("vector", lambda e: e.tensor_mul(nm_a[:, :], m_a[:, :],
              rs_a[:, :]), inc="sDVE")
        P.add("vector", lambda e: e.tensor_scalar(nm_a[:, :], nm_a[:, :],
              -1.0, None, ALU.mult), inc="sDVE")

    def ffn_pass():
        prev_stt = P.val("sDVE")
        rv_hist = [0, 0]
        psT_ev = P.val("sDVE")
        for t in range(NT):
            par = t % 2
            xt = xs[:, 1024 * t:1024 * (t + 1)]
            if True:
                qv = P.add("scalar", lambda e, xt=xt, t=t, par=par: e.activation(
                    qf32[par], xt, AF.Identity, bias=nm_a[:, t:t + 1],
                    scale=rs_a[:, t:t + 1]), waits=[("sDVE", rv_hist[par])],
                    inc="sACT")
                rv = P.add("vector", lambda e, par=par: e.tensor_scalar(
                    qb_t[par], qf32[par], MAGIC, MAGIC, ALU.add, ALU.subtract),
                    waits=[("sACT", qv)], inc="sDVE")
            rv_hist[par] = rv
            evs = []
            for g in range(2):
                tv = P.add("tensor", lambda e, g=g, par=par: [
                    e.transpose(psT[:, 128 * j:128 * (j + 1)],
                                qb_t[par][:, (4 * g + j) * 128:(4 * g + j + 1) * 128],
                                identb[:, :]) for j in range(4)][-1],
                    waits=[("sDVE", max(rv, psT_ev))], inc="sPE")
                psT_ev = P.add("vector", lambda e, g=g, par=par: e.tensor_copy(
                    qT_t[par][:, 512 * g:512 * (g + 1)], psT[:, 0:512]),
                    waits=[("sPE", tv)], inc="sDVE")
                evs.append(psT_ev)
            stts = [prev_stt]
            for h2 in range(2):
                ps = psC if h2 == 0 else psD
                mm = P.add("tensor", lambda e, par=par, h2=h2, ps=ps: [
                    e.matmul(ps[:, :],
                             qT_t[par][:, 128 * k:128 * (k + 1)],
                             big[:, 1024 * k + 512 * h2:
                                 1024 * k + 512 * (h2 + 1)],
                             start=(k == 0), stop=False) for k in range(8)][-1],
                    waits=[("sDVE", max(evs + [prev_stt]))], inc="sPE")
                mm = P.add("tensor", lambda e, h2=h2, ps=ps: e.matmul(
                    ps[:, :], oneb[:, :], biasq[:, 512 * h2:512 * (h2 + 1)],
                    start=False, stop=True), inc="sPE")
                stts.append(P.add("vector", lambda e, xt=xt, h2=h2, ps=ps:
                    e.scalar_tensor_tensor(
                        xt[:, 512 * h2:512 * (h2 + 1)], ps[:, :], sc[:, 8:9],
                        xt[:, 512 * h2:512 * (h2 + 1)], ALU.mult, ALU.add),
                    waits=[("sPE", mm)], inc="sDVE"))
            prev_stt = max(stts)

    P.barrier()
    if STAGE >= 0.3:
        ln_stats(minmax=True)
        iscale_collective()
        P.barrier()
    if STAGE >= 1:
        ffn_pass()
        P.barrier()

    if STAGE >= 2:
        for k in range(8):
            dma("sync", qwT_dram[:, 1024 * k:1024 * (k + 1)],
                big[:, 1024 * k:1024 * (k + 1)], fence=(k == 7))
        qwT_saved = P.val("sIN")
        ln_stats(minmax=False)
        rsd = P.val("sACT")
        P.add("vector", lambda e: e.tensor_mul(nm_a[:, :], m_a[:, :],
              rs_a[:, :]), waits=[("sACT", rsd)], inc="sDVE")
        st_done = P.add("vector", lambda e: e.tensor_scalar(nm_a[:, :],
                        nm_a[:, :], -1.0, None, ALU.mult), inc="sDVE")
        ln2dma = []
        for t in range(NT):
            par = t % 2
            xt = xs[:, 1024 * t:1024 * (t + 1)]
            w = [("sDVE", st_done)]
            if t >= 2:
                w.append(("sIN", ln2dma[t - 2]))
            av = P.add("scalar", lambda e, xt=xt, t=t, par=par: e.activation(
                ln2st[:, 1024 * par:1024 * (par + 1)], xt, AF.Identity,
                bias=nm_a[:, t:t + 1], scale=rs_a[:, t:t + 1]),
                waits=w, inc="sACT")
            d = dma("sync", ln2_in[128 * t:128 * (t + 1), :],
                    ln2st[:, 1024 * par:1024 * (par + 1)],
                    waits=[("sACT", av)], fence=(t >= NT - 2))
            ln2dma.append(d)
        P.add("gpsimd", lambda e: e.collective_compute(
            "AllGather", ALU.bypass, replica_groups=[[0, 1], [2, 3], [4, 5], [6, 7]],
            ins=[ln2_in.ap().opt()], outs=[ln2_out.ap().opt()]),
            waits=[("sIN", ln2dma[-1]), ("sAUX", P.val("sAUX"))], inc="sCC")
        ag1 = P.val("sCC")
        wic = 0
        for f in range(8):
            d = dma("sync", wist, wi_my[128 * f:128 * (f + 1), :],
                    waits=[("sIN", qwT_saved), ("sDVE", wic)])
            wic = P.add("vector", lambda e, f=f: e.tensor_copy(
                big[:, 1536 * f:1536 * (f + 1)], wist),
                waits=[("sIN", d)], inc="sDVE")
        bd_ = dma("sync", birow, bi_my[:, :], waits=[("sDVE", wic)])
        bcv = P.add("vector", lambda e: e.tensor_copy(birb, birow),
                    waits=[("sIN", bd_)], inc="sDVE")
        btr = P.add("tensor", lambda e: [
            e.transpose(psT[0:128, 2 * j:2 * j + 1],
                        birb[:, 128 * j:128 * (j + 1)],
                        identb[0:1, 0:1]) for j in range(8)][-1],
            waits=[("sDVE", bcv)], inc="sPE")
        P.add("vector", lambda e: e.tensor_copy(
            bcol[:, :], psT.ap()[:, 0:16].rearrange("p (j two) -> p j two",
                                                    two=2)[:, :, 0]),
            waits=[("sPE", btr)], inc="sDVE")
        P.add("vector", lambda e: e.tensor_scalar(bcol[:, 0:4], bcol[:, 0:4],
              0.125, None, ALU.mult), inc="sDVE")
        P.barrier()

        glob_tr = []
        for bd in range(NB):
            for tt in range(4):
                par = tt % 2
                g = 4 * bd + tt
                w = [("sCC", ag1)]
                if g >= 2:
                    w.append(("sPE", glob_tr[g - 2]))
                d = P.add("gpsimd", lambda e, g=g, par=par: e.indirect_dma_start(
                    ln2st[:, 1024 * par:1024 * (par + 1)], None,
                    ln2_out.ap(), bass.IndirectOffsetOnAxis(idxl[:, g:g + 1], 0)),
                    waits=w, inc="sAUX", by=16)
                for gg in range(2):
                    tv = P.add("tensor", lambda e, gg=gg, par=par: [
                        e.transpose(psT[:, 128 * j:128 * (j + 1)],
                                    ln2st[:, 1024 * par + (4 * gg + j) * 128:
                                          1024 * par + (4 * gg + j + 1) * 128],
                                    identb[:, :]) for j in range(4)][-1],
                        waits=[("sAUX", d), ("sDVE", P.val("sDVE"))], inc="sPE")
                    P.add("vector", lambda e, gg=gg, tt=tt: e.tensor_copy(
                        bass.AP(big, 12288 + (4 * gg) * 512 + 128 * tt,
                                [[16384, 128], [512, 4], [1, 128]]),
                        psT.ap()[:, 0:512].rearrange("p (j c) -> p j c", j=4)),
                        waits=[("sPE", tv)], inc="sDVE")
                glob_tr.append(P.val("sPE"))
            lnt_done = P.val("sDVE")
            for oc in range(8):
                ps = psC if oc % 2 == 0 else psD
                mm = P.add("tensor", lambda e, oc=oc, ps=ps: [
                    e.matmul(ps[:, :],
                             big[:, 1536 * f + 128 * oc:1536 * f + 128 * (oc + 1)],
                             big[:, 12288 + 512 * f:12288 + 512 * (f + 1)],
                             start=(f == 0), stop=(f == 7)) for f in range(8)][-1],
                    waits=[("sDVE", lnt_done), ("sACT", P.val("sACT"))],
                    inc="sPE")
                isq = oc < 4
                dest = (qt if isq else kt)
                doff = (oc if isq else oc - 4) * 4096 + 512 * bd
                P.add("scalar", lambda e, ps=ps, dest=dest, doff=doff, oc=oc,
                      isq=isq: e.activation(
                          dest[:, doff:doff + 512], ps[:, :], AF.Identity,
                          bias=bcol[:, oc:oc + 1], scale=(0.125 if isq else 1.0)),
                      waits=[("sPE", mm)], inc="sACT")
            for tt in range(4):
                ps = psE if tt % 2 == 0 else psF
                kc = 4 * bd + tt
                mm = P.add("tensor", lambda e, tt=tt, ps=ps: [
                    e.matmul(ps[:, :],
                             big[:, 12288 + 512 * f + 128 * tt:
                                  12288 + 512 * f + 128 * (tt + 1)],
                             big[:, 1536 * f + 1024:1536 * (f + 1)],
                             start=(f == 0), stop=False) for f in range(8)][-1],
                    waits=[("sDVE", P.val("sDVE"))], inc="sPE")
                mm = P.add("tensor", lambda e, ps=ps: e.matmul(
                    ps[:, :], oneb[:, :], bvrow, start=False, stop=True),
                    inc="sPE")
                P.add("vector", lambda e, ps=ps, kc=kc: e.tensor_copy(
                    bass.AP(vb, 65 * kc, [[8 * 32 * 65, 128], [2080, 8], [1, 64]]),
                    ps.ap().rearrange("p (j c) -> p j c", j=8)),
                    waits=[("sPE", mm)], inc="sDVE")
        P.barrier()

        for pp in range(4):
            for bd in range(NB):
                kmax = 4 * bd + 4
                qbase = 4096 * pp + 512 * bd
                exp_done = [0] * (kmax + 1)
                pool_done = [0] * (kmax + 1)
                av_done = [0, 0]
                for kc in range(kmax):
                    j0 = max(0, kc - 4 * bd)
                    qo = 128 * j0
                    par = kc % 2
                    psCk = psC if par == 0 else psA.ap()[:, 0:512]
                    psDk = psD if par == 0 else psA.ap()[:, 512:1024]
                    eA = expb[:, 0:512]
                    eB = expb[:, 512:1024]
                    w = [("sACT", exp_done[kc - 2] if kc >= 2 else 0),
                         ("sPE", av_done[0])]
                    sA = P.add("tensor", lambda e, pp=pp, kc=kc, qo=qo,
                               qbase=qbase, psCk=psCk: e.matmul(
                                   psCk[:, qo:512],
                                   kt[0:64, 4096 * pp + 128 * kc:
                                      4096 * pp + 128 * (kc + 1)],
                                   qt[0:64, qbase + qo:qbase + 512],
                                   start=True, stop=True), waits=w, inc="sPE")
                    sB = P.add("tensor", lambda e, pp=pp, kc=kc, qo=qo,
                               qbase=qbase, psDk=psDk: e.matmul(
                                   psDk[:, qo:512],
                                   kt[64:128, 4096 * pp + 128 * kc:
                                      4096 * pp + 128 * (kc + 1)],
                                   qt[64:128, qbase + qo:qbase + 512],
                                   start=True, stop=True,
                                   tile_position=(64, 0)), inc="sPE")
                    we = [("sPE", sA)]
                    if kc >= 2:
                        we.append(("sPOOL", pool_done[kc - 2]))
                    pspair = psCD if par == 0 else psA
                    ed = P.add("scalar", lambda e, qo=qo, pspair=pspair:
                               e.activation(
                                   bass.AP(expb, qo,
                                           [[1024, 128], [512, 2],
                                            [1, 512 - qo]]),
                                   bass.AP(pspair, qo,
                                           [[1024, 128], [512, 2],
                                            [1, 512 - qo]]),
                                   AF.Exp),
                               waits=we + [("sPE", sB)], inc="sACT")
                    exp_done[kc] = ed
                    wv = [("sACT", ed)]
                    if kc >= 4 * bd:
                        P.add("gpsimd", lambda e, eA=eA, qo=qo: e.affine_select(
                            eA[:, qo:qo + 128], eA[:, qo:qo + 128], [[1, 128]],
                            ALU.is_ge, 0.0, base=0, channel_multiplier=-1),
                            waits=[("sACT", ed)], inc="sPOOL")
                        pd = P.add("gpsimd", lambda e, eB=eB, qo=qo:
                                   e.affine_select(
                                       eB[:, qo:qo + 128], eB[:, qo:qo + 128],
                                       [[1, 128]], ALU.is_ge, 0.0, base=0,
                                       channel_multiplier=-1), inc="sPOOL")
                        pool_done[kc] = pd
                        wv.append(("sPOOL", pd))
                    else:
                        pool_done[kc] = pool_done[kc - 1] if kc else 0
                    P.add("tensor", lambda e, pp=pp, kc=kc, qo=qo, eA=eA,
                          kmax=kmax: e.matmul(
                              psE[0:65, qo:512],
                              vb[:, 2080 * (2 * pp) + 65 * kc:
                                 2080 * (2 * pp) + 65 * kc + 65],
                              eA[:, qo:512],
                              start=(kc == 0), stop=(kc == kmax - 1)),
                          waits=wv, inc="sPE")
                    av_done[par] = P.add("tensor", lambda e, pp=pp, kc=kc,
                        qo=qo, eB=eB, kmax=kmax: e.matmul(
                            psF[0:65, qo:512],
                            vb[:, 2080 * (2 * pp + 1) + 65 * kc:
                               2080 * (2 * pp + 1) + 65 * kc + 65],
                            eB[:, qo:512],
                            start=(kc == 0), stop=(kc == kmax - 1)),
                        inc="sPE")
                last = P.val("sPE")

                def lp(fn):
                    def g(e):
                        with nc.allow_low_precision("softmax denom in bf16"):
                            return fn(e)
                    return g
                recf = ln2st.bitcast(F32)
                P.add("scalar", lambda e: e.activation(
                    recf[0:1, 0:512], psE[64:65, :], AF.Ln),
                    waits=[("sPE", last)], inc="sACT")
                P.add("scalar", lambda e: e.activation(
                    recf[0:1, 512:1024], psF[64:65, :], AF.Ln), inc="sACT")
                P.add("scalar", lp(lambda e: e.activation(
                    rD[0:1, 0:1024], recf[0:1, 0:1024], AF.Exp, scale=-1.0)),
                    inc="sACT")
                P.add("tensor", lambda e: e.matmul(
                    psC[0:64, :], oneb[0:1, 0:64], rD[0:1, 0:512],
                    start=True, stop=True),
                    waits=[("sACT", P.val("sACT"))], inc="sPE")
                b2 = P.add("tensor", lambda e: e.matmul(
                    psC[64:128, :], oneb[0:1, 0:64], rD[0:1, 512:1024],
                    start=True, stop=True, tile_position=(0, 64)), inc="sPE")
                rbv = P.add("vector", lambda e: e.tensor_copy(rb[:, :],
                            psC[:, :]), waits=[("sPE", b2)], inc="sDVE")
                P.add("vector", lambda e, qbase=qbase: e.tensor_mul(
                    big[0:64, qbase:qbase + 512], psE[0:64, :], rb[0:64, :]),
                    waits=[("sDVE", rbv)], inc="sDVE")
                n2 = P.add("vector", lambda e, qbase=qbase: e.tensor_mul(
                    big[64:128, qbase:qbase + 512], psF[0:64, :],
                    rb[64:128, :]), inc="sDVE")
                P.pend("tensor", "sDVE", n2)
        P.barrier()

        for p4 in range(4):
            for hh in range(2):
                dma("sync", ot_in[128 * (4 * hh + p4):128 * (4 * hh + p4 + 1), :],
                    big[:, 4096 * p4 + 2048 * hh:4096 * p4 + 2048 * (hh + 1)],
                    fence=(p4 == 3 and hh == 1))
        otd = P.val("sIN")
        P.add("gpsimd", lambda e: e.collective_compute(
            "AllGather", ALU.bypass, replica_groups=[[0, 1], [2, 3], [4, 5], [6, 7]],
            ins=[ot_in.ap().opt()], outs=[ot_out.ap().opt()]),
            waits=[("sIN", otd), ("sAUX", P.val("sAUX"))], inc="sCC")
        ag2 = P.val("sCC")
        got = 0
        for g in range(8):
            dest = kt if g < 4 else qt
            doff = 2048 * (g % 4)
            got = P.add("gpsimd", lambda e, g=g, dest=dest, doff=doff:
                        e.indirect_dma_start(
                            dest[:, doff:doff + 2048], None, ot_out.ap(),
                            bass.IndirectOffsetOnAxis(idxo[:, g:g + 1], 0)),
                        waits=[("sCC", ag2)], inc="sAUX", by=16)
            if g == 7:
                P.pend("gpsimd", "sAUX", got)
        woc = 0
        for j in range(8):
            d = dma("sync", worow, wo_cat[128 * j:128 * (j + 1), :],
                    waits=[("sIN", otd), ("sDVE", woc)])
            woc = P.add("vector", lambda e, j=j: e.tensor_copy(
                qt[:, 8192 + 1024 * j:8192 + 1024 * (j + 1)], worow),
                waits=[("sIN", d)], inc="sDVE")
        bod = dma("sync", worow[0:1, :], bo[:, :], waits=[("sDVE", woc)])
        bocast = P.add("vector", lambda e: e.tensor_copy(borow[:, :],
                       worow[0:1, :]), waits=[("sIN", bod)], inc="sDVE")
        P.barrier()

        prev = max(P.val("sDVE"), bocast)
        for t in range(NT):
            mm = P.add("tensor", lambda e, t=t: [
                e.matmul(psA[:, 512 * h2:512 * (h2 + 1)],
                         (kt if j < 4 else qt)[:, 2048 * (j % 4) + 128 * t:
                                               2048 * (j % 4) + 128 * (t + 1)],
                         qt[:, 8192 + 1024 * j + 512 * h2:
                            8192 + 1024 * j + 512 * (h2 + 1)],
                         start=(j == 0), stop=False)
                for j in range(8) for h2 in range(2)][-1],
                waits=[("sAUX", got), ("sDVE", prev)], inc="sPE")
            mm = P.add("tensor", lambda e: [
                e.matmul(psA[:, 512 * h2:512 * (h2 + 1)], oneb[:, :],
                         borow[:, 512 * h2:512 * (h2 + 1)],
                         start=False, stop=True) for h2 in range(2)][-1],
                inc="sPE")
            prev = P.add("vector", lambda e, t=t: [
                e.tensor_add(
                    xs[:, 1024 * t + 512 * h2:1024 * t + 512 * (h2 + 1)],
                    xs[:, 1024 * t + 512 * h2:1024 * t + 512 * (h2 + 1)],
                    psA[:, 512 * h2:512 * (h2 + 1)]) for h2 in range(2)][-1],
                waits=[("sPE", mm)], inc="sDVE")
        P.barrier()

    if STAGE >= 3:
        for k in range(8):
            dma("sync", big[:, 1024 * k:1024 * (k + 1)],
                qwT_dram[:, 1024 * k:1024 * (k + 1)], fence=(k == 7))
        qw2 = dma("sync", fbrow, fb[:, :])
        ln_stats(minmax=True)
        iscale_collective()
        P.barrier()
        P.pend("tensor", "sIN", qw2)
        ffn_pass()
        P.barrier()

    for t in range(NT):
        dma("sync", out_my[128 * t:128 * (t + 1), :],
            xs[:, 1024 * t:1024 * (t + 1)], fence=False)
    P.mark("sync", "sSN", waits=[("sIN", P.val("sIN"))])

    return _finish(nc, P, es)


def _finish(nc, P, es):
    with (
        nc.semaphore("sIN") as sIN, nc.semaphore("sAUX") as sAUX,
        nc.semaphore("sCC") as sCC, nc.semaphore("sPE") as sPE,
        nc.semaphore("sDVE") as sDVE, nc.semaphore("sACT") as sACT,
        nc.semaphore("sPOOL") as sPOOL, nc.semaphore("sSN") as sSN,
        nc.semaphore("sGN") as sGN,
        nc.Block() as block,
    ):
        sems = dict(sIN=sIN, sAUX=sAUX, sCC=sCC, sPE=sPE, sDVE=sDVE,
                    sACT=sACT, sPOOL=sPOOL, sSN=sSN, sGN=sGN)
        P.emit(block, sems)
    es.close()
    return nc


_NC_CACHE = {}


def kernel(L0_attn_x, L0_attn_w_in, L0_attn_b_in, L0_attn_w_out, L0_attn_b_out,
           L0_ffn_weight, L0_ffn_bias):
    x = np.ascontiguousarray(np.asarray(L0_attn_x, dtype=np.float32))
    w_in = np.ascontiguousarray(np.asarray(L0_attn_w_in, dtype=np.float32))
    b_in = np.asarray(L0_attn_b_in, dtype=np.float32).reshape(1, 3072)
    w_out = np.ascontiguousarray(np.asarray(L0_attn_w_out, dtype=np.float32))
    b_out = np.asarray(L0_attn_b_out, dtype=np.float32).reshape(1, D)
    fw = np.ascontiguousarray(np.asarray(L0_ffn_weight, dtype=np.float32))
    fb = np.asarray(L0_ffn_bias, dtype=np.float32).reshape(1, D)

    in_maps = []
    for c in range(8):
        b, h = c // 2, c % 2
        qs = slice(512 * h, 512 * h + 512)
        ks = slice(1024 + 512 * h, 1024 + 512 * h + 512)
        vs = slice(2048 + 512 * h, 2048 + 512 * h + 512)
        wi = np.ascontiguousarray(
            np.concatenate([w_in[:, qs], w_in[:, ks], w_in[:, vs]], axis=1))
        bi = np.ascontiguousarray(
            np.concatenate([b_in[:, qs], b_in[:, ks], b_in[:, vs]], axis=1))
        wo = np.ascontiguousarray(np.concatenate(
            [w_out[512 * h:512 * h + 512, :],
             w_out[512 * (1 - h):512 * (1 - h) + 512, :]], axis=0))
        idxl_np = np.arange(4096, dtype=np.uint32).reshape(32, 128).T.copy()
        rows = np.concatenate([
            h * 1024 + 512 * h + np.arange(512, dtype=np.uint32),
            (1 - h) * 1024 + 512 * h + np.arange(512, dtype=np.uint32)])
        idxo_np = rows.reshape(8, 128).T.copy()
        in_maps.append(dict(
            x_my=np.ascontiguousarray(x[b, 2048 * h:2048 * h + 2048, :]),
            wi_my=wi, bi_my=bi, wo_cat=wo, bo=b_out,
            fw=fw, fb=fb, idx_ln2=idxl_np, idx_ot=idxo_np))

    if "nc" not in _NC_CACHE:
        _NC_CACHE["nc"] = build_nc()
    res = run_bass_kernel_spmd(_NC_CACHE["nc"], in_maps, core_ids=list(range(8)))
    out = np.empty((B, S, D), dtype=np.float32)
    for c in range(8):
        b, h = c // 2, c % 2
        out[b, 2048 * h:2048 * h + 2048, :] = res.results[c]["out_my"]
    return out
